# version 02
# baseline (speedup 1.0000x reference)
import numpy as np
import ml_dtypes
import concourse.bass as bass
import concourse.mybir as mybir
from concourse.bass_utils import run_bass_kernel_spmd

F32 = mybir.dt.float32
BF16 = mybir.dt.bfloat16
AF = mybir.ActivationFunctionType
ALU = mybir.AluOpType
AX = mybir.AxisListType

NC = 8
T = 2048
D = 1024
KT = 8
E = 32
EPS = 1e-5
LOG_LO = float(np.log(1e-8))
LOG_HI = float(np.log1p(-1e-8))

ENGINES = ("tensor", "vector", "scalar", "gpsimd", "sync")

STAGES = ["ln1", "inproj", "scan1", "carry", "scan2", "attn_a",
          "ln2", "routing1", "ffn1", "attn_b", "ffn2", "full"]


class Plan:
    def __init__(self):
        self.stream = {e: [] for e in ENGINES}
        self.semv = {}
        self.semh = {}
        self.bank_prod = [None] * 8
        self.bank_cons = [[] for _ in range(8)]

    def wait(self, eng, sem, val):
        if val <= 0:
            return
        self.stream[eng].append(lambda h, s=sem, v=val: h.wait_ge(self.semh[s], v))

    def op(self, eng, makefn, inc=None, n=1):
        if inc is not None:
            self.semv[inc] = self.semv.get(inc, 0) + n
            def fn(h, mk=makefn, s=inc, k=n):
                mk(h).then_inc(self.semh[s], k)
        else:
            def fn(h, mk=makefn):
                mk(h)
        self.stream[eng].append(fn)
        return self.semv.get(inc, 0)

    def val(self, sem):
        return self.semv.get(sem, 0)

    def bank_write(self, banks):
        for b in banks:
            for (s, v) in self.bank_cons[b]:
                self.wait("tensor", s, v)
            self.bank_cons[b] = []

    def bank_produced(self, banks, val):
        for b in banks:
            self.bank_prod[b] = ("sPE", val)

    def bank_read(self, eng, banks, marker_sem):
        for b in banks:
            if self.bank_prod[b] is not None:
                s, v = self.bank_prod[b]
                self.wait(eng, s, v)
        mv = self.val(marker_sem) + 1
        for b in banks:
            self.bank_cons[b].append((marker_sem, mv))


def build(debug_stage="full", taps=()):
    stage_idx = STAGES.index(debug_stage)

    def stage_on(name):
        return STAGES.index(name) <= stage_idx

    nc = bass.Bass("TRN2", target_bir_lowering=False, debug=False)
    p = Plan()
    taps = set(taps)
    tapmeta = {}

    def din(name, shape, dt=F32):
        return nc.dram_tensor(name, shape, dt, kind="ExternalInput").ap()

    x_in = din("x_in", [D, T])
    in_wT = din("in_wT", [D, 2 * D])
    out_wT = din("out_wT", [D, D])
    wbT = din("wbT", [D, D])
    wcT = din("wcT", [D, D])
    wdT = din("wdT", [D, D])
    gate_wT = din("gate_wT", [D, E])
    wgT = din("wgT", [E, D, 512])
    wuT = din("wuT", [E, D, 512])
    wdnT = din("wdnT", [E, 512, D])
    shgT = din("shgT", [D, D])
    shuT = din("shuT", [D, D])
    shdT = din("shdT", [D, D])
    A_all_in = din("A_all", [128, 128])
    B_all_in = din("B_all", [128, 128])
    C_all_in = din("C_all", [128, 128])
    LDT_all_in = din("LDT_all", [128, 128])
    in_b_c = din("in_b_c", [128, 16])
    out_b_c = din("out_b_c", [128, 8])
    ln_g_c = din("ln_g_c", [128, 8])
    ln_b_c = din("ln_b_c", [128, 8])
    logA_b_c = din("logA_b_c", [128, 8])
    logdt_b = din("logdt_b", [1, 1])
    bB_c = din("bB_c", [128, 8])
    bC_c = din("bC_c", [128, 8])
    bD_c = din("bD_c", [128, 8])
    gate_b_r = din("gate_b_r", [1, E])
    e_vec = din("e_vec", [128, 8])
    m_vec = din("m_vec", [128, 8])
    ident_in = din("ident_in", [128, 128])
    beye_in = din("beye_in", [128, 8])

    y_out = nc.dram_tensor("y_out", [D, T], F32, kind="ExternalOutput").ap()

    ag_a_in = nc.dram_tensor("ag_a_in", [128, 128], F32)
    ag_a_out = nc.dram_tensor("ag_a_out", [NC * 128, 128], F32)
    ag_b_in = nc.dram_tensor("ag_b_in", [128, 8], F32)
    ag_b_out = nc.dram_tensor("ag_b_out", [NC * 128, 8], F32)

    from contextlib import ExitStack
    ctx = ExitStack()
    e_ = ctx.enter_context

    def sbt(name, shape, dt):
        return e_(nc.sbuf_tensor(name, shape, dt))

    X_ALL = sbt("X_ALL", [128, KT * T], F32)
    XN_ALL = sbt("XN_ALL", [128, KT * T], BF16)
    XZ_ALL = sbt("XZ_ALL", [128, 16 * T], BF16)
    UNION = sbt("UNION", [128, 6144], F32)

    def Xt(k):
        return X_ALL[:, T * k:T * (k + 1)]

    def XNt(k):
        return XN_ALL[:, T * k:T * (k + 1)]

    def XZt(j):
        return XZ_ALL[:, T * j:T * (j + 1)]

    def WRslot(s):
        return XZ_ALL[:, 4096 * s:4096 * (s + 1)]

    UB = UNION.bitcast(BF16)

    def HBt(j):
        return UB[:, T * j:T * (j + 1)]

    LOGFM = UNION[0:32, 0:2048]
    SCORES = UNION[:, 2048:2560]
    SCALL = UNION[:, 2560:3072]
    MASKED = UNION[:, 3072:3584]
    SEL = UNION[:, 3584:4096]
    CWT = UNION[:, 4096:4608]
    GSALL = UNION[:, 4608:4736]
    GMASK = UNION[:, 4736:4864]
    IND = UNION[:, 4864:4992]
    M16 = UNION[:, 4992:5008]
    W16 = UNION[:, 5008:5024]
    RC16 = UNION[:, 5024:5040]
    CFM = XZ_ALL[0:32, 29696:31744]

    WSL = [sbt(f"WSL{j}", [128, 1024], BF16) for j in range(2)]
    GW = sbt("GW", [128, E * KT], BF16)
    PARN = ["ABAR", "CBDT", "LOGAB", "LDT", "SSELF", "HA", "SJ0", "SJ1",
            "SCR0", "SCR1"]
    PAR = {n: sbt(n, [128, 128], F32) for n in PARN}
    BEYE = sbt("BEYE", [128, 8], F32)
    MB = [sbt(f"MB{j}", [128, 128], BF16) for j in range(2)]
    SM_N = ["OUTB", "LNG", "LNB", "LOGABB", "ABARB", "BBDT", "BCD", "BDC2",
            "SBB", "HBB", "EV", "MV", "SJB", "SJB2"]
    SM = {n: sbt(n, [128, 8], F32) for n in SM_N}
    INB = sbt("INB", [128, 16], F32)
    BIAS2 = sbt("BIAS2", [128, 16], F32)
    DTCOL = sbt("DTCOL", [128, 1], F32)
    IDENT = sbt("IDENT", [128, 128], F32)
    ONESC = sbt("ONESC", [128, 1], F32)
    EPSC = sbt("EPSC", [1, 1], F32)
    ONESR = sbt("ONESR", [1, 128], F32)
    ONESRB = sbt("ONESRB", [1, 128], BF16)
    GBR = sbt("GBR", [1, E], F32)
    GBF = sbt("GBF", [128, E], F32)
    LNBBF = sbt("LNBBF", [128, 8], BF16)
    XSQ = [sbt(f"XSQ{j}", [128, 512], F32) for j in range(2)]
    TMPB = [sbt(f"TMP{j}", [128, 512], F32) for j in range(2)]
    ROWT = sbt("ROWT", [128, 512], F32)
    R_M = ROWT[0:1, :]
    R_VAR = TMPB[0][0:1, :]
    R_L1 = TMPB[1][0:1, :]
    R_T = XSQ[0][0:1, :]
    R_RC = XSQ[1][0:1, :]

    SROW = UNION[0:1, 0:2048]
    BROW = UNION[0:1, 2048:4096]
    RHOROW = UNION[0:1, 4096:6144]
    GSB = [XZ_ALL[:, 24576 + 512 * j:24576 + 512 * (j + 1)] for j in range(2)]
    TBF = [XZ_ALL[:, 25600 + 512 * j:25600 + 512 * (j + 1)] for j in range(2)]
    PRB = [XZ_ALL[:, 26624 + 512 * j:26624 + 512 * (j + 1)] for j in range(4)]
    BCP = [XZ_ALL[:, 28672 + 512 * j:28672 + 512 * (j + 1)] for j in range(2)]

    BK = [e_(nc.psum_tensor(f"BK{b}", [128, 512], F32)) for b in range(8)]

    semnames = ["sPE", "sV", "sS", "sG", "sX", "sPar", "sParG",
                "sHin0", "sHin1", "sSl0", "sSl1",
                "sW0", "sW1", "sW2", "sW3", "sW4", "sW5",
                "sAGi", "sAG", "sAGo", "sStore"]
    semh = {n: e_(nc.semaphore(n)) for n in semnames}
    p.semh = semh
    for n in semnames:
        p.semv[n] = 0

    def mm(out, lhsT, rhs, start, stop, banks, inc="stop"):
        if start:
            p.bank_write(banks)
        do_inc = (inc == "always") or (inc == "stop" and stop)
        v = p.op("tensor", lambda h, o=out, l=lhsT, r=rhs, st=start, sp=stop:
                 h.matmul(o, lhsT=l, rhs=r, start=st, stop=sp),
                 inc=("sPE" if do_inc else None))
        if stop and do_inc:
            p.bank_produced(banks, v)
        return v

    def transpose(out, in_, banks):
        p.bank_write(banks)
        kk = in_.shape[0]
        v = p.op("tensor", lambda h, o=out, i=in_, k=kk: h.transpose(
            o, i, IDENT[0:k, 0:k]), inc="sPE")
        p.bank_produced(banks, v)
        return v

    def vop(makefn, read_banks=()):
        if read_banks:
            p.bank_read("vector", read_banks, "sV")
        return p.op("vector", makefn, inc="sV")

    def sop(makefn, read_banks=()):
        if read_banks:
            p.bank_read("scalar", read_banks, "sS")
        return p.op("scalar", makefn, inc="sS")

    def gop(makefn):
        return p.op("gpsimd", makefn, inc="sG")

    def dma(eng, out, in_, sem):
        return p.op(eng, lambda h, o=out, i=in_: h.dma_start(out=o, in_=i),
                    inc=sem, n=16)

    def tap(name, ap, shape, dt):
        if name not in taps:
            return
        t_ = nc.dram_tensor("tap_" + name, list(shape), dt, kind="ExternalOutput").ap()
        tapmeta[name] = True
        for s in ("sV", "sS", "sPE", "sG"):
            p.wait("sync", s, p.val(s))
        if ap.dtype != dt:
            p.wait("gpsimd", "sV", p.val("sV"))
            p.wait("gpsimd", "sS", p.val("sS"))
            dma("gpsimd", t_, ap, "sStore")
        else:
            dma("sync", t_, ap, "sStore")

    def finish():
        for s in ("sV", "sS", "sPE", "sG"):
            p.wait("sync", s, p.val(s))
        for k in range(KT):
            dma("sync", y_out[128 * k:128 * (k + 1), :], Xt(k), "sStore")
        p.wait("sync", "sStore", p.val("sStore"))
        with nc.Block() as block:
            for eng in ENGINES:
                dec = getattr(block, eng)

                def make_body(eng=eng):
                    def body(h):
                        for fn in p.stream[eng]:
                            fn(h)
                    return body
                dec(make_body())
        ctx.close()
        return nc, tapmeta

    for k in range(KT):
        dma("sync", Xt(k), x_in[128 * k:128 * (k + 1), :], "sX")
    dma("sync", PAR["ABAR"][:], A_all_in, "sPar")
    dma("sync", PAR["CBDT"][:], B_all_in, "sPar")
    dma("sync", PAR["LOGAB"][:], C_all_in, "sPar")
    dma("sync", PAR["LDT"][:], LDT_all_in, "sPar")
    dma("sync", INB[:], in_b_c, "sPar")
    dma("sync", IDENT[:], ident_in, "sPar")
    dma("sync", BEYE[:], beye_in, "sPar")
    dma("sync", GBR[:], gate_b_r, "sPar")
    dma("sync", DTCOL[0:1, 0:1], logdt_b, "sPar")
    for nm, src in [("OUTB", out_b_c), ("LNG", ln_g_c), ("LNB", ln_b_c),
                    ("LOGABB", logA_b_c), ("BBDT", bB_c), ("BCD", bC_c),
                    ("BDC2", bD_c), ("EV", e_vec), ("MV", m_vec)]:
        dma("sync", SM[nm][:], src, "sPar")
    PAR_TOT = p.val("sPar")
    dma("gpsimd", GW[:].rearrange("pk (a e) -> pk a e", e=E),
        gate_wT.rearrange("(a pk) e -> pk a e", pk=128), "sParG")
    dma("gpsimd", LNBBF[:], ln_b_c, "sParG")
    PARG_TOT = p.val("sParG")

    gop(lambda h: h.memset(ONESC[:], 1.0))
    gop(lambda h: h.memset(EPSC[:], EPS))
    gop(lambda h: h.memset(ONESR[:], 1.0))
    gop(lambda h: h.memset(ONESRB[:], 1.0))
    gop(lambda h: h.memset(MB[0][:], 0.0))
    gop(lambda h: h.memset(MB[1][:], 0.0))
    gop(lambda h: h.memset(PAR["HA"][:], 0.0))
    gop(lambda h: h.memset(SM["HBB"][:], 0.0))
    G_SETUP = p.val("sG")

    p.wait("scalar", "sPar", PAR_TOT)
    p.wait("vector", "sPar", PAR_TOT)
    sop(lambda h: h.activation(out=PAR["SCR0"][:], in_=PAR["LDT"][:], func=AF.Exp))
    p.wait("vector", "sS", p.val("sS"))
    vop(lambda h: h.tensor_scalar(out=PAR["SCR0"][:], in0=PAR["SCR0"][:],
                                  scalar1=1e-4, scalar2=1.0, op0=ALU.max, op1=ALU.min))
    p.wait("vector", "sV", p.val("sV"))
    vop(lambda h: h.tensor_tensor(out=PAR["CBDT"][:], in0=PAR["CBDT"][:],
                                  in1=PAR["LOGAB"][:], op=ALU.mult))
    p.wait("vector", "sV", p.val("sV"))
    vop(lambda h: h.tensor_tensor(out=PAR["CBDT"][:], in0=PAR["CBDT"][:],
                                  in1=PAR["SCR0"][:], op=ALU.mult))
    sop(lambda h: h.activation(out=PAR["SCR1"][:], in_=PAR["ABAR"][:], func=AF.Exp))
    p.wait("vector", "sS", p.val("sS"))
    vop(lambda h: h.tensor_tensor(out=PAR["SCR1"][:], in0=PAR["SCR1"][:],
                                  in1=PAR["SCR0"][:], op=ALU.mult))
    p.wait("vector", "sV", p.val("sV"))
    vop(lambda h: h.tensor_scalar(out=PAR["SCR1"][:], in0=PAR["SCR1"][:],
                                  scalar1=-1.0, op0=ALU.mult, scalar2=None))
    p.wait("vector", "sV", p.val("sV"))
    vop(lambda h: h.tensor_scalar(out=PAR["LOGAB"][:], in0=PAR["SCR1"][:],
                                  scalar1=LOG_LO, scalar2=LOG_HI,
                                  op0=ALU.max, op1=ALU.min))
    p.wait("scalar", "sV", p.val("sV"))
    sop(lambda h: h.activation(out=PAR["ABAR"][:], in_=PAR["LOGAB"][:], func=AF.Exp))
    sop(lambda h: h.activation(out=DTCOL[0:1, 0:1], in_=DTCOL[0:1, 0:1], func=AF.Exp))
    p.wait("tensor", "sS", p.val("sS"))
    p.wait("tensor", "sG", G_SETUP)
    mm(BK[0][:, 0:1], ONESR[:], DTCOL[0:1, 0:1], True, True, [0])
    vop(lambda h: h.tensor_copy(DTCOL[:], BK[0][:, 0:1]), read_banks=[0])
    sop(lambda h: h.activation(out=SM["ABARB"][:], in_=SM["LOGABB"][:], func=AF.Exp))
    p.wait("vector", "sS", p.val("sS"))
    vop(lambda h: h.tensor_scalar(out=SM["LOGABB"][:], in0=SM["ABARB"][:],
                                  scalar1=DTCOL[:, 0:1], op0=ALU.mult,
                                  scalar2=-1.0, op1=ALU.mult))
    p.wait("scalar", "sV", p.val("sV"))
    sop(lambda h: h.activation(out=SM["ABARB"][:], in_=SM["LOGABB"][:], func=AF.Exp))
    vop(lambda h: h.tensor_scalar(out=SM["BBDT"][:], in0=SM["BBDT"][:],
                                  scalar1=DTCOL[:, 0:1], op0=ALU.mult, scalar2=None))
    vop(lambda h: h.tensor_tensor(out=SM["BCD"][:], in0=SM["BCD"][:],
                                  in1=SM["BDC2"][:], op=ALU.add))
    mm(BK[1][:, 0:E], ONESR[:], GBR[:], True, True, [1])
    vop(lambda h: h.tensor_copy(GBF[:], BK[1][:, 0:E]), read_banks=[1])

    def ln_phase(kind):
        p.wait("scalar", "sG", G_SETUP)
        p.wait("scalar", "sX", 16 * KT)
        p.wait("tensor", "sX", 16 * KT)
        p.wait("scalar", "sV", p.val("sV"))
        p.wait("tensor", "sV", p.val("sV"))
        for tc in range(4):
            cs = slice(512 * tc, 512 * (tc + 1))
            for k in range(KT):
                buf = XSQ[k % 2]
                sv = sop(lambda h, b=buf, k=k, c=cs: h.activation(
                    out=b[:], in_=Xt(k)[:, c], func=AF.Square))
                p.wait("tensor", "sS", sv)
                mm(BK[0][0:1, :], ONESC[:], Xt(k)[:, cs], k == 0, k == KT - 1, [0],
                   inc="stop")
                vmm = mm(BK[1][0:1, :], ONESC[:], buf[:], k == 0, k == KT - 1, [1],
                         inc="always")
                if k + 2 < KT:
                    p.wait("scalar", "sPE", vmm)
            vop(lambda h: h.tensor_scalar(out=R_M, in0=BK[0][0:1, :],
                                          scalar1=1.0 / D, op0=ALU.mult, scalar2=None),
                read_banks=[0])
            vop(lambda h: h.tensor_scalar(out=R_T, in0=BK[1][0:1, :],
                                          scalar1=1.0 / D, op0=ALU.mult, scalar2=None),
                read_banks=[1])
            p.wait("vector", "sV", p.val("sV"))
            vop(lambda h: h.tensor_tensor(out=R_RC, in0=R_M, in1=R_M,
                                          op=ALU.mult))
            p.wait("vector", "sV", p.val("sV"))
            vop(lambda h: h.tensor_tensor(out=R_VAR, in0=R_T, in1=R_RC,
                                          op=ALU.subtract))
            p.wait("scalar", "sV", p.val("sV"))
            sop(lambda h: h.activation(out=R_L1, in_=R_VAR, func=AF.Ln,
                                       bias=EPSC[0:1, 0:1]))
            if kind == "ln1":
                vop(lambda h: h.tensor_scalar(out=R_T, in0=R_VAR, scalar1=EPS,
                                              op0=ALU.add, scalar2=None))
                p.wait("vector", "sV", p.val("sV"))
                vop(lambda h: h.reciprocal(R_RC, R_T))
                p.wait("vector", "sV", p.val("sV"))
                vop(lambda h: h.tensor_tensor(out=R_VAR, in0=R_VAR, in1=R_RC,
                                              op=ALU.mult))
                p.wait("scalar", "sV", p.val("sV"))
                sop(lambda h: h.activation(out=R_T, in_=R_VAR, func=AF.Ln,
                                           bias=EPSC[0:1, 0:1]))
                p.wait("vector", "sS", p.val("sS"))
                vop(lambda h: h.tensor_tensor(out=R_RC, in0=R_L1, in1=R_T,
                                              op=ALU.add))
                p.wait("scalar", "sV", p.val("sV"))
                sop(lambda h, c=cs: h.activation(out=SROW[:, c], in_=R_RC,
                                                 func=AF.Exp, scale=-0.5))
                sop(lambda h, c=cs: h.activation(out=RHOROW[:, c], in_=R_T,
                                                 func=AF.Exp, scale=0.5))
            else:
                sop(lambda h, c=cs: h.activation(out=SROW[:, c], in_=R_L1,
                                                 func=AF.Exp, scale=-0.5))
            p.wait("vector", "sS", p.val("sS"))
            vop(lambda h, c=cs: h.tensor_tensor(out=BROW[:, c], in0=R_M,
                                                in1=SROW[:, c], op=ALU.mult))
            p.wait("vector", "sV", p.val("sV"))
            vop(lambda h, c=cs: h.tensor_scalar(out=BROW[:, c], in0=BROW[:, c],
                                                scalar1=-1.0, op0=ALU.mult,
                                                scalar2=None))
            ba, bb = (2, 3) if tc % 2 == 0 else (4, 5)
            p.wait("tensor", "sV", p.val("sV"))
            mm(BK[ba][:], ONESR[:], SROW[0:1, cs], True, True, [ba])
            mm(BK[bb][:], ONESR[:], BROW[0:1, cs], True, True, [bb])
            for k in range(KT):
                vop(lambda h, k=k, c=cs, b=ba: h.tensor_tensor(
                    out=TMPB[k % 2][:], in0=Xt(k)[:, c], in1=BK[b][:], op=ALU.mult),
                    read_banks=[ba])
                p.wait("vector", "sV", p.val("sV"))
                vop(lambda h, k=k, c=cs, b=bb: h.tensor_tensor(
                    out=XNt(k)[:, c], in0=TMPB[k % 2][:], in1=BK[b][:], op=ALU.add),
                    read_banks=[bb])

    ln_phase("ln1")
    tap("xn1", XN_ALL[:, 0:T], [128, T], F32)
    if not stage_on("inproj"):
        return finish()

    def load_wslice(src2d, m, slot, sem):
        ap = src2d[:, 128 * m:128 * (m + 1)].rearrange("(kt pk) m -> pk kt m",
                                                       pk=128)
        dst = WSL[slot][:].rearrange("pk (kt m) -> pk kt m", m=128)
        dma("gpsimd", dst, ap, sem)
        return p.val(sem)

    slot_pe = [0, 0]
    LN1_V = p.val("sV")
    p.wait("tensor", "sV", LN1_V)

    def inproj_like(src2d, dst_t, bias_mode, mrange):
        for m in mrange:
            slot = m % 2
            ssem = "sSl0" if slot == 0 else "sSl1"
            p.wait("gpsimd", "sPE", slot_pe[slot])
            dval = load_wslice(src2d, m, slot, ssem)
            if bias_mode == "bias2silu":
                bb = 4 + (m % 2)
                p.wait("tensor", ssem, dval)
                if m == 0:
                    p.wait("tensor", "sParG", PARG_TOT)
                for k in range(KT):
                    mm(BK[bb][:, m:m + 1], WSL[slot][:, 128 * k:128 * (k + 1)],
                       LNBBF[:, k:k + 1], k == 0, k == KT - 1, [bb])
                sop(lambda h, m=m, b=bb: h.activation(
                    out=BIAS2[:, m:m + 1], in_=BK[b][:, m:m + 1], func=AF.Identity,
                    bias=INB[:, m:m + 1], scale=1.0), read_banks=[bb])
                p.wait("gpsimd", ssem, dval)
                p.wait("gpsimd", "sPE", p.val("sPE"))
                for k in range(KT):
                    gop(lambda h, s=slot, k=k: h.tensor_scalar(
                        out=WSL[s][:, 128 * k:128 * (k + 1)],
                        in0=WSL[s][:, 128 * k:128 * (k + 1)],
                        scalar1=SM["LNG"][:, k:k + 1], op0=ALU.mult, scalar2=None))
                p.wait("tensor", "sG", p.val("sG"))
            else:
                p.wait("tensor", ssem, dval)
            for tc in range(4):
                bk = 6 if (m * 4 + tc) % 2 == 0 else 7
                cs = slice(512 * tc, 512 * (tc + 1))
                for k in range(KT):
                    mm(BK[bk][:], WSL[slot][:, 128 * k:128 * (k + 1)],
                       XNt(k)[:, cs], k == 0, k == KT - 1, [bk])
                if bias_mode == "bias2silu":
                    fn = AF.Identity if m < 8 else AF.Silu
                    sop(lambda h, m=m, c=cs, b=bk, f=fn: h.activation(
                        out=dst_t(m)[:, c], in_=BK[b][:], func=f,
                        bias=BIAS2[:, m:m + 1], scale=1.0), read_banks=[bk])
                else:
                    sop(lambda h, m=m, c=cs, b=bk: h.activation(
                        out=dst_t(m)[:, c], in_=BK[b][:], func=AF.Identity,
                        bias=SM["BBDT"][:, m:m + 1], scale=DTCOL[:, 0:1]),
                        read_banks=[bk])
            slot_pe[slot] = p.val("sPE")

    inproj_like(in_wT, XZt, "bias2silu", range(16))
    XP_DONE_S = p.val("sS")
    tap("xp0", XZ_ALL[:, 0:T], [128, T], F32)
    tap("z0", XZ_ALL[:, 8 * T:9 * T], [128, T], F32)
    if not stage_on("scan1"):
        return finish()

    def expand_dma(t, hb):
        mt, s = t // 16, t % 16
        src = XZt(mt)[8 * s:8 * s + 8, :].unsqueeze(1).broadcast_to([8, 16, T])
        sem = "sHin0" if hb == 0 else "sHin1"
        dma("sync", HBt(hb), src, sem)
        return sem, p.val(sem)

    hb_scan_done = [0, 0]
    ho_cons = [0, 0]
    p.wait("sync", "sS", XP_DONE_S)
    for t in range(128):
        hb = t % 2
        p.wait("sync", "sV", hb_scan_done[hb])
        sem, dv = expand_dma(t, hb)
        p.wait("vector", sem, dv)
        p.wait("vector", "sV", ho_cons[hb])
        vop(lambda h, t=t, s=hb: h.tensor_tensor_scan(
            out=HBt(2 + s), data0=PAR["ABAR"][:, t:t + 1].broadcast_to([128, T]),
            data1=HBt(s), initial=0.0, op0=ALU.mult, op1=ALU.add))
        hb_scan_done[hb] = p.val("sV")
        p.wait("vector", "sV", hb_scan_done[hb])
        vop(lambda h, t=t, s=hb: h.tensor_copy(PAR["SSELF"][:, t:t + 1],
                                               HBt(2 + s)[:, T - 1:T]))
        ho_cons[hb] = p.val("sV")
        if t == 5:
            tap("hexp5", HBt(hb), [128, T], F32)
            tap("hscan5", HBt(2 + hb), [128, T], F32)
    tap("sself", PAR["SSELF"][:], [128, 128], F32)
    if not stage_on("carry"):
        return finish()

    def allgather_carry(ssrc, ag_in, ag_out, width, logab_ap, hdst, sjbufs):
        p.wait("sync", "sV", p.val("sV"))
        dma("sync", ag_in.ap(), ssrc, "sAGi")
        p.wait("gpsimd", "sAGi", p.val("sAGi"))
        p.op("gpsimd", lambda h, ai=ag_in, ao=ag_out: h.collective_compute(
            "AllGather", ALU.bypass, replica_groups=[list(range(NC))],
            ins=[ai.ap().opt()], outs=[ao.ap().opt()]), inc="sAG")
        agv = p.val("sAG")
        p.wait("sync", "sAG", agv)
        sj_cons = [0, 0]
        for j in range(7):
            buf = sjbufs[j % 2]
            p.wait("sync", "sV", sj_cons[j % 2])
            dma("sync", buf, ag_out.ap()[128 * j:128 * (j + 1), :], "sAGo")
            do = p.val("sAGo")
            scr = PAR["SCR0"][:, 0:width]
            scr1 = PAR["SCR1"][:, 0:width]
            vop(lambda h, la=logab_ap, s=scr, j=j: h.tensor_scalar(
                out=s, in0=la, scalar1=SM["EV"][:, j:j + 1], op0=ALU.mult,
                scalar2=None))
            p.wait("scalar", "sV", p.val("sV"))
            sop(lambda h, s=scr: h.activation(out=s, in_=s, func=AF.Exp, scale=2048.0))
            p.wait("vector", "sS", p.val("sS"))
            p.wait("vector", "sAGo", do)
            vop(lambda h, s=scr, s1=scr1, b=buf: h.tensor_tensor(
                out=s1, in0=s, in1=b, op=ALU.mult))
            sj_cons[j % 2] = p.val("sV")
            p.wait("vector", "sV", p.val("sV"))
            vop(lambda h, s1=scr1, j=j, hd=hdst: h.scalar_tensor_tensor(
                out=hd, in0=s1, scalar=SM["MV"][:, j:j + 1], in1=hd,
                op0=ALU.mult, op1=ALU.add))

    allgather_carry(PAR["SSELF"][:], ag_a_in, ag_a_out, 128, PAR["LOGAB"][:],
                    PAR["HA"][:], [PAR["SJ0"][:], PAR["SJ1"][:]])
    tap("ha", PAR["HA"][:], [128, 128], F32)
    if not stage_on("scan2"):
        return finish()

    hb_scan_done = [0, 0]
    fold_done = [0, 0]
    mb_used = [0, 0]
    for P_ in range(8):
        ybanks = [0, 1, 2, 3] if P_ % 2 == 0 else [4, 5, 6, 7]
        for g in range(16):
            t = 16 * P_ + g
            hb = t % 2
            p.wait("sync", "sV", hb_scan_done[hb])
            sem, dv = expand_dma(t, hb)
            mbuf = MB[t % 2]
            p.wait("gpsimd", "sPE", mb_used[t % 2])
            if t >= 2:
                gprev = (t - 2) % 16
                gop(lambda h, mb=mbuf, gp=gprev: h.memset(
                    mb[:, 8 * gp:8 * gp + 8], 0.0))
            gop(lambda h, mb=mbuf, g=g, t=t: h.tensor_scalar(
                out=mb[:, 8 * g:8 * g + 8], in0=BEYE[:],
                scalar1=PAR["CBDT"][:, t:t + 1], op0=ALU.mult, scalar2=None))
            gv = p.val("sG")
            p.wait("vector", sem, dv)
            p.wait("vector", "sPE", fold_done[hb])
            vop(lambda h, t=t, s=hb: h.tensor_tensor_scan(
                out=HBt(2 + s), data0=PAR["ABAR"][:, t:t + 1].broadcast_to([128, T]),
                data1=HBt(s), initial=PAR["HA"][:, t:t + 1],
                op0=ALU.mult, op1=ALU.add))
            hb_scan_done[hb] = p.val("sV")
            scan_v = hb_scan_done[hb]
            p.wait("tensor", "sG", gv)
            p.wait("tensor", "sV", scan_v)
            for q in range(4):
                mm(BK[ybanks[q]][:], mbuf[:], HBt(2 + hb)[:, 512 * q:512 * (q + 1)],
                   g == 0, g == 15, [ybanks[q]], inc="always")
            fold_done[hb] = p.val("sPE")
            mb_used[t % 2] = p.val("sPE")
        for q in range(4):
            vop(lambda h, P_=P_, q=q, b=ybanks[q]: h.tensor_copy(
                XZt(P_)[:, 512 * q:512 * (q + 1)], BK[b][:]), read_banks=[ybanks[q]])
    tap("y0", XZ_ALL[:, 0:T], [128, T], F32)
    if not stage_on("attn_a"):
        return finish()

    p.wait("vector", "sS", XP_DONE_S)
    for k in range(KT):
        vop(lambda h, k=k: h.tensor_tensor(out=XZt(k), in0=XZt(k), in1=XZt(8 + k),
                                           op=ALU.mult))
    PROD_V = p.val("sV")
    p.wait("tensor", "sV", PROD_V)
    for tc in range(4):
        mm(BK[tc][:], ONESR[:], RHOROW[0:1, 512 * tc:512 * (tc + 1)], True, True,
           [tc])
    for m in range(KT):
        slot = m % 2
        ssem = "sSl0" if slot == 0 else "sSl1"
        p.wait("gpsimd", "sPE", slot_pe[slot])
        dval = load_wslice(out_wT, m, slot, ssem)
        p.wait("tensor", ssem, dval)
        for tc in range(4):
            bk = 5 if (m * 4 + tc) % 2 == 0 else 7
            cs = slice(512 * tc, 512 * (tc + 1))
            for k in range(KT):
                mm(BK[bk][:], WSL[slot][:, 128 * k:128 * (k + 1)],
                   XZt(k)[:, cs], k == 0, k == KT - 1, [bk])
            vop(lambda h, m=m, c=cs, tc=tc: h.tensor_tensor(
                out=TMPB[0][:], in0=XNt(m)[:, c], in1=BK[tc][:], op=ALU.mult),
                read_banks=[tc])
            p.wait("vector", "sV", p.val("sV"))
            vop(lambda h, m=m, b=bk: h.scalar_tensor_tensor(
                out=TMPB[1][:], in0=BK[b][:], scalar=SM["OUTB"][:, m:m + 1],
                in1=TMPB[0][:], op0=ALU.add, op1=ALU.add), read_banks=[bk])
            p.wait("vector", "sV", p.val("sV"))
            vop(lambda h: h.tensor_scalar(out=TMPB[1][:], in0=TMPB[1][:],
                                          scalar1=10.0, scalar2=-10.0,
                                          op0=ALU.min, op1=ALU.max))
            p.wait("vector", "sV", p.val("sV"))
            vop(lambda h, m=m, c=cs: h.tensor_tensor(
                out=Xt(m)[:, c], in0=Xt(m)[:, c], in1=TMPB[1][:], op=ALU.add))
        slot_pe[slot] = p.val("sPE")
    tap("x_attn_a", X_ALL[:, 0:T], [128, T], F32)
    if not stage_on("ln2"):
        return finish()

    def vopd(makefn, read_banks=()):
        p.wait("vector", "sV", p.val("sV"))
        return vop(makefn, read_banks)

    def moe_block(callidx):
        MOE_ENTRY_PE = p.val("sPE")
        MOE_ENTRY_V = p.val("sV")
        p.wait("tensor", "sV", p.val("sV"))
        for tc4 in range(4):
            cs = slice(512 * tc4, 512 * (tc4 + 1))
            bk = 0 if tc4 % 2 == 0 else 1
            if callidx == 0 and tc4 == 0:
                p.wait("tensor", "sParG", PARG_TOT)
            for k in range(KT):
                mm(BK[bk][0:32, :], GW[:, 32 * k:32 * (k + 1)], XNt(k)[:, cs],
                   k == 0, k == KT - 1, [bk])
            vopd(lambda h, c=cs, b=bk: h.tensor_copy(LOGFM[:, c], BK[b][0:32, :]),
                read_banks=[bk])
        p.wait("tensor", "sV", p.val("sV"))
        for j in range(16):
            bk = 2 if j % 2 == 0 else 3
            transpose(BK[bk][0:128, 0:32], LOGFM[:, 128 * j:128 * (j + 1)], [bk])
            sop(lambda h, j=j, b=bk: h.activation(
                out=SCORES[:, 32 * j:32 * (j + 1)], in_=BK[b][0:128, 0:32],
                func=AF.Sigmoid), read_banks=[bk])
        p.wait("vector", "sS", p.val("sS"))
        vopd(lambda h: h.tensor_tensor(
            out=SCALL.rearrange("p (a b) -> p a b", b=32),
            in0=SCORES.rearrange("p (a b) -> p a b", b=32),
            in1=GBF[:].unsqueeze(1).broadcast_to([128, 16, 32]), op=ALU.add))
        s4 = SCALL.rearrange("p (a g r) -> p a g r", g=8, r=4)
        pairs = [(0, 1), (0, 2), (0, 3), (1, 2), (1, 3), (2, 3)]
        for idx, (a, b) in enumerate(pairs):
            if idx == 0:
                vopd(lambda h, a=a, b=b: h.tensor_tensor(
                    out=GSALL.rearrange("p (x g) -> p x g", g=8),
                    in0=s4[:, :, :, a], in1=s4[:, :, :, b], op=ALU.add))
            else:
                vopd(lambda h, a=a, b=b: h.tensor_tensor(
                    out=IND.rearrange("p (x g) -> p x g", g=8),
                    in0=s4[:, :, :, a], in1=s4[:, :, :, b], op=ALU.add))
                p.wait("vector", "sV", p.val("sV"))
                vopd(lambda h: h.tensor_tensor(out=GSALL[:], in0=GSALL[:],
                                              in1=IND[:], op=ALU.max))
        for it in range(4):
            vopd(lambda h: h.tensor_reduce(
                M16[:], GSALL.rearrange("p (x g) -> p x g", g=8), axis=AX.X,
                op=ALU.max))
            p.wait("vector", "sV", p.val("sV"))
            vopd(lambda h: h.tensor_tensor(
                out=IND.rearrange("p (x g) -> p x g", g=8),
                in0=GSALL.rearrange("p (x g) -> p x g", g=8),
                in1=M16.unsqueeze(2).broadcast_to([128, 16, 8]), op=ALU.is_ge))
            p.wait("vector", "sV", p.val("sV"))
            if it == 0:
                vopd(lambda h: h.tensor_copy(GMASK[:], IND[:]))
            else:
                vopd(lambda h: h.tensor_tensor(out=GMASK[:], in0=GMASK[:],
                                              in1=IND[:], op=ALU.add))
            vopd(lambda h: h.scalar_tensor_tensor(
                out=GSALL[:], in0=IND[:], scalar=-1e30, in1=GSALL[:],
                op0=ALU.mult, op1=ALU.add))
        vopd(lambda h: h.tensor_tensor(
            out=MASKED.rearrange("p (x g r) -> p x g r", g=8, r=4),
            in0=s4,
            in1=GMASK.rearrange("p (x g) -> p x g", g=8).unsqueeze(3)
                .broadcast_to([128, 16, 8, 4]), op=ALU.mult))
        for it in range(8):
            vopd(lambda h: h.tensor_reduce(
                M16[:], MASKED.rearrange("p (x b) -> p x b", b=32), axis=AX.X,
                op=ALU.max))
            p.wait("vector", "sV", p.val("sV"))
            tgx = SEL if it == 0 else CWT
            vopd(lambda h, tg=tgx: h.tensor_tensor(
                out=tg.rearrange("p (x b) -> p x b", b=32),
                in0=MASKED.rearrange("p (x b) -> p x b", b=32),
                in1=M16.unsqueeze(2).broadcast_to([128, 16, 32]), op=ALU.is_ge))
            p.wait("vector", "sV", p.val("sV"))
            if it > 0:
                vopd(lambda h: h.tensor_tensor(out=SEL[:], in0=SEL[:], in1=CWT[:],
                                              op=ALU.add))
            vopd(lambda h, tg=tgx: h.scalar_tensor_tensor(
                out=MASKED[:], in0=tg[:], scalar=-1e30, in1=MASKED[:],
                op0=ALU.mult, op1=ALU.add))
        vopd(lambda h: h.tensor_tensor(out=CWT[:], in0=SCORES[:], in1=SEL[:],
                                      op=ALU.mult))
        vopd(lambda h: h.tensor_reduce(
            W16[:], CWT.rearrange("p (x b) -> p x b", b=32), axis=AX.X, op=ALU.add))
        vopd(lambda h: h.tensor_scalar(out=W16[:], in0=W16[:], scalar1=1e-20,
                                      op0=ALU.add, scalar2=None))
        p.wait("vector", "sV", p.val("sV"))
        vopd(lambda h: h.reciprocal(RC16[:], W16[:]))
        vopd(lambda h: h.tensor_scalar(out=RC16[:], in0=RC16[:], scalar1=2.5,
                                      op0=ALU.mult, scalar2=None))
        p.wait("vector", "sV", p.val("sV"))
        vopd(lambda h: h.tensor_tensor(
            out=CWT.rearrange("p (x b) -> p x b", b=32),
            in0=CWT.rearrange("p (x b) -> p x b", b=32),
            in1=RC16.unsqueeze(2).broadcast_to([128, 16, 32]), op=ALU.mult))
        p.wait("tensor", "sV", p.val("sV"))
        for j in range(16):
            bk = 2 if j % 2 == 0 else 3
            transpose(BK[bk][0:32, 0:128], CWT[:, 32 * j:32 * (j + 1)], [bk])
            vopd(lambda h, j=j, b=bk: h.tensor_copy(
                CFM[:, 128 * j:128 * (j + 1)], BK[b][0:32, 0:128]),
                read_banks=[bk])
        ROUTE_V = p.val("sV")
        tap(f"combine{callidx}", CFM, [32, T], F32)
        tap(f"scores{callidx}", SCORES, [128, 512], F32)
        tap(f"gmask{callidx}", GMASK, [128, 128], F32)
        tap(f"sel{callidx}", SEL, [128, 512], F32)
        tap(f"cwt{callidx}", CWT, [128, 512], F32)
        if callidx == 0 and not stage_on("ffn1"):
            return False
        p.wait("tensor", "sV", ROUTE_V)
        bcp_used = [ROUTE_V, ROUTE_V]
        gsb_used = [0, 0]
        slot_free = [0] * 6
        wbase = {i: p.val(f"sW{i}") for i in range(6)}
        wring_pe = [max(MOE_ENTRY_PE, 1)] * 6
        wring_v = [MOE_ENTRY_V] * 6
        ldq = []
        for pr in range(16):
            ldq += [("g", 2 * pr), ("g", 2 * pr + 1), ("u", 2 * pr),
                    ("u", 2 * pr + 1), ("d", 2 * pr), ("d", 2 * pr + 1)]
        ldq += [("sg", 0), ("sg", 1), ("su", 0), ("su", 1), ("sd", 0), ("sd", 1)]
        nld = [0]

        def issue_load():
            if nld[0] >= len(ldq):
                return
            kind, idx = ldq[nld[0]]
            slotidx = nld[0] % 6
            sem = f"sW{slotidx}"
            p.wait("gpsimd", "sPE", wring_pe[slotidx])
            p.wait("gpsimd", "sV", wring_v[slotidx])
            if kind == "g":
                ap = wgT[idx].rearrange("(kt pk) i -> pk kt i", pk=128)
            elif kind == "u":
                ap = wuT[idx].rearrange("(kt pk) i -> pk kt i", pk=128)
            elif kind == "d":
                ap = wdnT[idx].rearrange("(it pk) m -> pk it m", pk=128)
            elif kind == "sg":
                ap = shgT[:, 512 * idx:512 * (idx + 1)].rearrange(
                    "(kt pk) i -> pk kt i", pk=128)
            elif kind == "su":
                ap = shuT[:, 512 * idx:512 * (idx + 1)].rearrange(
                    "(kt pk) i -> pk kt i", pk=128)
            else:
                ap = shdT[512 * idx:512 * (idx + 1), :].rearrange(
                    "(it pk) m -> pk it m", pk=128)
            nfree = ap.shape[1] * ap.shape[2]
            dst = WRslot(slotidx).rearrange("pk (a b) -> pk a b", a=ap.shape[1])
            dma("gpsimd", dst, ap, sem)
            nld[0] += 1

        for _ in range(6):
            issue_load()

        for pr in range(17):
            shared = (pr == 16)
            wv = {f"sW{i}": wbase[i] + 16 * (pr + 1) for i in range(6)}
            sl_g0, sl_g1, sl_u0, sl_u1, sl_d0, sl_d1 = range(6)
            for tc in range(8):
                c2 = slice(256 * tc, 256 * (tc + 1))
                bsem = "sHin0" if tc % 2 == 0 else "sHin1"
                if not shared:
                    p.wait("sync", "sV", bcp_used[tc % 2])
                    dma("sync", BCP[tc % 2][:, 0:256],
                        CFM[2 * pr:2 * pr + 1, c2].unsqueeze(1)
                        .broadcast_to([1, 128, 256]), bsem)
                    dma("sync", BCP[tc % 2][:, 256:512],
                        CFM[2 * pr + 1:2 * pr + 2, c2].unsqueeze(1)
                        .broadcast_to([1, 128, 256]), bsem)
                bcdv = p.val(bsem)
                for it in range(4):
                    bG = 0 if it % 2 == 0 else 2
                    bU = 1 if it % 2 == 0 else 3
                    p.wait("tensor", "sW0", wv["sW0"])
                    p.wait("tensor", "sW1", wv["sW1"])
                    p.bank_write([bG])
                    for k in range(KT):
                        mm(BK[bG][:, 0:256],
                           WRslot(sl_g0)[:, 512 * k + 128 * it:512 * k + 128 * (it + 1)],
                           XNt(k)[:, c2], k == 0, False, [bG], inc=None)
                    for k in range(KT):
                        vv = mm(BK[bG][:, 256:512],
                                WRslot(sl_g1)[:, 512 * k + 128 * it:512 * k + 128 * (it + 1)],
                                XNt(k)[:, c2], k == 0, k == KT - 1, [bG],
                                inc=("always" if k == KT - 1 else None))
                    p.bank_produced([bG], vv)
                    if tc == 7 and it == 3:
                        slot_free[sl_g0] = slot_free[sl_g1] = vv
                    p.wait("scalar", "sV", gsb_used[it % 2])
                    sop(lambda h, b=bG, pp=it % 2: h.activation(
                        out=GSB[pp][:], in_=BK[b][:], func=AF.Silu), read_banks=[bG])
                    gsv = p.val("sS")
                    p.wait("tensor", "sW2", wv["sW2"])
                    p.wait("tensor", "sW3", wv["sW3"])
                    p.bank_write([bU])
                    for k in range(KT):
                        mm(BK[bU][:, 0:256],
                           WRslot(sl_u0)[:, 512 * k + 128 * it:512 * k + 128 * (it + 1)],
                           XNt(k)[:, c2], k == 0, False, [bU], inc=None)
                    for k in range(KT):
                        vv = mm(BK[bU][:, 256:512],
                                WRslot(sl_u1)[:, 512 * k + 128 * it:512 * k + 128 * (it + 1)],
                                XNt(k)[:, c2], k == 0, k == KT - 1, [bU],
                                inc=("always" if k == KT - 1 else None))
                    p.bank_produced([bU], vv)
                    p.wait("vector", "sS", gsv)
                    vop(lambda h, b=bU, pp=it % 2: h.tensor_tensor(
                        out=TBF[pp][:], in0=GSB[pp][:], in1=BK[b][:], op=ALU.mult),
                        read_banks=[bU])
                    gsb_used[it % 2] = p.val("sV")
                    p.wait("vector", "sV", p.val("sV"))
                    if shared:
                        vop(lambda h, it=it, pp=it % 2: h.tensor_copy(
                            PRB[it][:], TBF[pp][:]))
                    else:
                        p.wait("vector", bsem, bcdv)
                        vop(lambda h, it=it, pp=it % 2, bp=tc % 2: h.tensor_tensor(
                            out=PRB[it][:], in0=TBF[pp][:], in1=BCP[bp][:],
                            op=ALU.mult))
                        bcp_used[tc % 2] = p.val("sV")
                PR_V = p.val("sV")
                if pr == 0 and tc == 0 and callidx == 0:
                    tap("bcp0", BCP[0][:], [128, 512], F32)
                    tap("gsb0", GSB[1][:], [128, 512], F32)
                    tap("prb0", PRB[0][:], [128, 512], F32)
                    tap("prb3", PRB[3][:], [128, 512], F32)
                p.wait("tensor", "sW4", wv["sW4"])
                p.wait("tensor", "sW5", wv["sW5"])
                p.wait("tensor", "sV", PR_V)
                p.bank_write([4, 5, 6, 7])
                last_v = 0
                for half, sld in ((0, sl_d0), (1, sl_d1)):
                    for m in range(KT):
                        bD = 4 + m // 2
                        dsl = BK[bD][:, 256 * (m % 2):256 * (m % 2) + 256]
                        for it in range(4):
                            first = (half == 0 and it == 0 and m % 2 == 0)
                            last = (half == 1 and it == 3)
                            last_v = mm(dsl,
                                        WRslot(sld)[:, 1024 * it + 128 * m:
                                                    1024 * it + 128 * (m + 1)],
                                        PRB[it][:, 256 * half:256 * (half + 1)],
                                        first, last, [bD],
                                        inc=("always" if last else None))
                p.bank_produced([4, 5, 6, 7], last_v)
                if tc == 7:
                    slot_free[sl_d0] = slot_free[sl_d1] = last_v
                for bb_ in range(4):
                    m0 = 2 * bb_
                    xap = X_ALL[:, T * m0:T * (m0 + 2)].rearrange(
                        "p (m t) -> p m t", m=2)[:, :, 256 * tc:256 * (tc + 1)]
                    vop(lambda h, b=4 + bb_, xa=xap: h.tensor_tensor(
                        out=xa, in0=xa,
                        in1=BK[b][:].rearrange("p (m t) -> p m t", m=2),
                        op=ALU.add), read_banks=[4 + bb_])
            done_v = p.val("sV")
            for i in range(6):
                wring_pe[i] = slot_free[i]
                wring_v[i] = done_v
            for _ in range(6):
                issue_load()
        return True

    ln_phase("plain")
    tap("hn1", XN_ALL[:, 0:T], [128, T], F32)
    if not stage_on("routing1"):
        return finish()
    cont = moe_block(0)
    tap("x_ffn1", X_ALL[:, 0:T], [128, T], F32)
    if not cont or not stage_on("attn_b"):
        return finish()

    ln_phase("plain")
    inproj_like(wbT, XZt, "bu", range(8))
    BU_S = p.val("sS")
    p.wait("vector", "sS", BU_S)
    for k in range(KT):
        vop(lambda h, k=k: h.tensor_tensor_scan(
            out=HBt(2), data0=SM["ABARB"][:, k:k + 1].broadcast_to([128, T]),
            data1=XZt(k), initial=0.0, op0=ALU.mult, op1=ALU.add))
        p.wait("vector", "sV", p.val("sV"))
        vop(lambda h, k=k: h.tensor_copy(SM["SBB"][:, k:k + 1], HBt(2)[:, T - 1:T]))
    allgather_carry(SM["SBB"][:], ag_b_in, ag_b_out, 8, SM["LOGABB"][:],
                    SM["HBB"][:], [SM["SJB"][:], SM["SJB2"][:]])
    for k in range(KT):
        vop(lambda h, k=k: h.tensor_tensor_scan(
            out=XZt(8 + k), data0=SM["ABARB"][:, k:k + 1].broadcast_to([128, T]),
            data1=XZt(k), initial=SM["HBB"][:, k:k + 1], op0=ALU.mult, op1=ALU.add))
    XS_V = p.val("sV")
    tap("xs0", XZ_ALL[:, 8 * T:9 * T], [128, T], F32)
    p.wait("tensor", "sV", XS_V)
    for m in range(KT):
        p.wait("gpsimd", "sPE", slot_pe[0])
        dc = load_wslice(wcT, m, 0, "sSl0")
        p.wait("gpsimd", "sPE", slot_pe[1])
        dd = load_wslice(wdT, m, 1, "sSl1")
        p.wait("tensor", "sSl0", dc)
        p.wait("tensor", "sSl1", dd)
        for tc in range(4):
            bk = 6 if (m * 4 + tc) % 2 == 0 else 7
            cs = slice(512 * tc, 512 * (tc + 1))
            for k in range(KT):
                mm(BK[bk][:], WSL[0][:, 128 * k:128 * (k + 1)], XZt(8 + k)[:, cs],
                   k == 0, False, [bk], inc=None)
            for k in range(KT):
                vv = mm(BK[bk][:], WSL[1][:, 128 * k:128 * (k + 1)], XNt(k)[:, cs],
                        False, k == KT - 1, [bk],
                        inc=("always" if k == KT - 1 else None))
            p.bank_produced([bk], vv)
            vop(lambda h, m=m, c=cs, b=bk: h.scalar_tensor_tensor(
                out=Xt(m)[:, c], in0=BK[b][:], scalar=SM["BCD"][:, m:m + 1],
                in1=Xt(m)[:, c], op0=ALU.add, op1=ALU.add), read_banks=[bk])
        slot_pe[0] = p.val("sPE")
        slot_pe[1] = p.val("sPE")
    tap("x_attn_b", X_ALL[:, 0:T], [128, T], F32)
    if not stage_on("ffn2"):
        return finish()

    ln_phase("plain")
    moe_block(1)
    tap("x_ffn2", X_ALL[:, 0:T], [128, T], F32)
    return finish()


def _prep_inputs(inputs):
    f32 = np.float32

    def c(a):
        return np.ascontiguousarray(a, dtype=f32)

    x = np.asarray(inputs["L0_attn_a_x"], dtype=f32)[0]
    com = {
        "in_wT": c(np.asarray(inputs["L0_attn_a_in_w"], dtype=f32).T),
        "out_wT": c(np.asarray(inputs["L0_attn_a_out_w"], dtype=f32).T),
        "wbT": c(np.asarray(inputs["L0_attn_b_W_B"], dtype=f32).T),
        "wcT": c(np.asarray(inputs["L0_attn_b_W_C"], dtype=f32).T),
        "wdT": c(np.asarray(inputs["L0_attn_b_W_D"], dtype=f32).T),
        "gate_wT": c(np.asarray(inputs["L0_ffn_gate_w"], dtype=f32).T),
        "wgT": c(np.asarray(inputs["L0_ffn_w_gate"], dtype=f32).transpose(0, 2, 1)),
        "wuT": c(np.asarray(inputs["L0_ffn_w_up"], dtype=f32).transpose(0, 2, 1)),
        "wdnT": c(np.asarray(inputs["L0_ffn_w_down"], dtype=f32).transpose(0, 2, 1)),
        "shgT": c(np.asarray(inputs["L0_ffn_sh_gate"], dtype=f32).T),
        "shuT": c(np.asarray(inputs["L0_ffn_sh_up"], dtype=f32).T),
        "shdT": c(np.asarray(inputs["L0_ffn_sh_down"], dtype=f32).T),
        "A_all": c(np.asarray(inputs["L0_attn_a_A_log"], dtype=f32)
                   .reshape(128, 128).T),
        "B_all": c(np.asarray(inputs["L0_attn_a_B_param"], dtype=f32)
                   .reshape(128, 128).T),
        "C_all": c(np.asarray(inputs["L0_attn_a_C_param"], dtype=f32)
                   .reshape(128, 128).T),
        "ident_in": np.eye(128, dtype=f32),
        "beye_in": (np.arange(128)[:, None] // 16 ==
                    np.arange(8)[None, :]).astype(f32),
        "gate_b_r": c(np.asarray(inputs["L0_ffn_gate_bias"], dtype=f32)
                      .reshape(1, E)),
        "logdt_b": c(np.asarray(inputs["L0_attn_b_log_dt"], dtype=f32)
                     .reshape(1, 1)),
    }
    ld = np.asarray(inputs["L0_attn_a_log_dt"], dtype=f32)
    idx = (8 * np.arange(128))[None, :] + (np.arange(128) // 16)[:, None]
    com["LDT_all"] = c(ld[idx])

    def col128(v):
        return c(np.asarray(v, dtype=f32).reshape(-1, 128).T)

    com["in_b_c"] = col128(inputs["L0_attn_a_in_b"])
    com["out_b_c"] = col128(inputs["L0_attn_a_out_b"])
    com["ln_g_c"] = col128(inputs["L0_attn_a_ln_g"])
    com["ln_b_c"] = col128(inputs["L0_attn_a_ln_b"])
    com["logA_b_c"] = col128(inputs["L0_attn_b_log_A"])
    com["bB_c"] = col128(inputs["L0_attn_b_b_B"])
    com["bC_c"] = col128(inputs["L0_attn_b_b_C"])
    com["bD_c"] = col128(inputs["L0_attn_b_b_D"])

    in_maps = []
    for cix in range(NC):
        m = dict(com)
        m["x_in"] = c(x[T * cix:T * (cix + 1), :].T)
        ev = np.zeros((128, 8), dtype=f32)
        mv = np.zeros((128, 8), dtype=f32)
        for j in range(7):
            if j < cix:
                ev[:, j] = cix - 1 - j
                mv[:, j] = 1.0
        m["e_vec"] = ev
        m["m_vec"] = mv
        in_maps.append(m)
    return in_maps


def kernel(**inputs):
    in_maps = _prep_inputs(inputs)
    nc, _ = build("full")
    res = run_bass_kernel_spmd(nc, in_maps, core_ids=list(range(NC)))
    parts = [np.asarray(r["y_out"]).T for r in res.results]
    out = np.concatenate(parts, axis=0).reshape(1, NC * T, D)
    return np.ascontiguousarray(out, dtype=np.float32)


# version 03
# speedup vs baseline: 1.2490x; 1.2490x over previous
import numpy as np
import ml_dtypes
import concourse.bass as bass
import concourse.mybir as mybir
from concourse.bass_utils import run_bass_kernel_spmd

F32 = mybir.dt.float32
BF16 = mybir.dt.bfloat16
AF = mybir.ActivationFunctionType
ALU = mybir.AluOpType
AX = mybir.AxisListType

NC = 8
T = 2048
D = 1024
KT = 8
E = 32
EPS = 1e-5
LOG_LO = float(np.log(1e-8))
LOG_HI = float(np.log1p(-1e-8))

ENGINES = ("tensor", "vector", "scalar", "gpsimd", "sync")

STAGES = ["ln1", "inproj", "scan1", "carry", "scan2", "attn_a",
          "ln2", "routing1", "ffn1", "attn_b", "ffn2", "full"]


class Plan:
    def __init__(self):
        self.stream = {e: [] for e in ENGINES}
        self.semv = {}
        self.semh = {}
        self.bank_prod = [None] * 8
        self.bank_cons = [[] for _ in range(8)]

    def wait(self, eng, sem, val):
        if val <= 0:
            return
        self.stream[eng].append(lambda h, s=sem, v=val: h.wait_ge(self.semh[s], v))

    def op(self, eng, makefn, inc=None, n=1):
        if inc is not None:
            self.semv[inc] = self.semv.get(inc, 0) + n
            def fn(h, mk=makefn, s=inc, k=n):
                mk(h).then_inc(self.semh[s], k)
        else:
            def fn(h, mk=makefn):
                mk(h)
        self.stream[eng].append(fn)
        return self.semv.get(inc, 0)

    def val(self, sem):
        return self.semv.get(sem, 0)

    def bank_write(self, banks):
        for b in banks:
            for (s, v) in self.bank_cons[b]:
                self.wait("tensor", s, v)
            self.bank_cons[b] = []

    def bank_produced(self, banks, val):
        for b in banks:
            self.bank_prod[b] = ("sPE", val)

    def bank_read(self, eng, banks, marker_sem):
        for b in banks:
            if self.bank_prod[b] is not None:
                s, v = self.bank_prod[b]
                self.wait(eng, s, v)
        mv = self.val(marker_sem) + 1
        for b in banks:
            self.bank_cons[b].append((marker_sem, mv))


def build(debug_stage="full", taps=()):
    stage_idx = STAGES.index(debug_stage)

    def stage_on(name):
        return STAGES.index(name) <= stage_idx

    nc = bass.Bass("TRN2", target_bir_lowering=False, debug=False)
    p = Plan()
    taps = set(taps)
    tapmeta = {}

    def din(name, shape, dt=F32):
        return nc.dram_tensor(name, shape, dt, kind="ExternalInput").ap()

    x_in = din("x_in", [D, T])
    in_wT = din("in_wT", [D, 2 * D])
    out_wT = din("out_wT", [D, D])
    wbT = din("wbT", [D, D])
    wcT = din("wcT", [D, D])
    wdT = din("wdT", [D, D])
    gate_wT = din("gate_wT", [D, E])
    wgT = din("wgT", [E, D, 512])
    wuT = din("wuT", [E, D, 512])
    wdnT = din("wdnT", [E, 512, D])
    shgT = din("shgT", [D, D])
    shuT = din("shuT", [D, D])
    shdT = din("shdT", [D, D])
    A_all_in = din("A_all", [128, 128])
    B_all_in = din("B_all", [128, 128])
    C_all_in = din("C_all", [128, 128])
    LDT_all_in = din("LDT_all", [128, 128])
    in_b_c = din("in_b_c", [128, 16])
    out_b_c = din("out_b_c", [128, 8])
    ln_g_c = din("ln_g_c", [128, 8])
    ln_b_c = din("ln_b_c", [128, 8])
    logA_b_c = din("logA_b_c", [128, 8])
    logdt_b = din("logdt_b", [1, 1])
    bB_c = din("bB_c", [128, 8])
    bC_c = din("bC_c", [128, 8])
    bD_c = din("bD_c", [128, 8])
    gate_b_r = din("gate_b_r", [1, E])
    e_vec = din("e_vec", [128, 8])
    m_vec = din("m_vec", [128, 8])
    ident_in = din("ident_in", [128, 128])
    beye_in = din("beye_in", [128, 8])

    y_out = nc.dram_tensor("y_out", [D, T], F32, kind="ExternalOutput").ap()

    ag_a_in = nc.dram_tensor("ag_a_in", [128, 128], F32)
    ag_a_out = nc.dram_tensor("ag_a_out", [NC * 128, 128], F32)
    ag_b_in = nc.dram_tensor("ag_b_in", [128, 8], F32)
    ag_b_out = nc.dram_tensor("ag_b_out", [NC * 128, 8], F32)

    from contextlib import ExitStack
    ctx = ExitStack()
    e_ = ctx.enter_context

    def sbt(name, shape, dt):
        return e_(nc.sbuf_tensor(name, shape, dt))

    X_ALL = sbt("X_ALL", [128, KT * T], F32)
    XN_ALL = sbt("XN_ALL", [128, KT * T], BF16)
    XZ_ALL = sbt("XZ_ALL", [128, 16 * T], BF16)
    UNION = sbt("UNION", [128, 6144], F32)

    def Xt(k):
        return X_ALL[:, T * k:T * (k + 1)]

    def XNt(k):
        return XN_ALL[:, T * k:T * (k + 1)]

    def XZt(j):
        return XZ_ALL[:, T * j:T * (j + 1)]

    def WRslot(s):
        return XZ_ALL[:, 4096 * s:4096 * (s + 1)]

    UB = UNION.bitcast(BF16)

    def HBt(j):
        return UB[:, T * j:T * (j + 1)]

    LOGFM = UNION[0:32, 0:2048]
    SCORES = UNION[:, 2048:2560]
    SCALL = UNION[:, 2560:3072]
    MASKED = UNION[:, 3072:3584]
    SEL = UNION[:, 3584:4096]
    CWT = UNION[:, 4096:4608]
    GSALL = UNION[:, 4608:4736]
    GMASK = UNION[:, 4736:4864]
    IND = UNION[:, 4864:4992]
    M16 = UNION[:, 4992:5008]
    W16 = UNION[:, 5008:5024]
    RC16 = UNION[:, 5024:5040]
    CFM = XZ_ALL[0:32, 29696:31744]

    WSL = [sbt(f"WSL{j}", [128, 1024], BF16) for j in range(2)]
    GW = sbt("GW", [128, E * KT], BF16)
    PARN = ["ABAR", "CBDT", "LOGAB", "LDT", "SSELF", "HA", "SJ0", "SJ1",
            "SCR0", "SCR1"]
    PAR = {n: sbt(n, [128, 128], F32) for n in PARN}
    BEYE = sbt("BEYE", [128, 8], F32)
    MB = [sbt(f"MB{j}", [128, 128], BF16) for j in range(2)]
    SM_N = ["OUTB", "LNG", "LNB", "LOGABB", "ABARB", "BBDT", "BCD", "BDC2",
            "SBB", "HBB", "EV", "MV", "SJB", "SJB2"]
    SM = {n: sbt(n, [128, 8], F32) for n in SM_N}
    INB = sbt("INB", [128, 16], F32)
    BIAS2 = sbt("BIAS2", [128, 16], F32)
    DTCOL = sbt("DTCOL", [128, 1], F32)
    IDENT = sbt("IDENT", [128, 128], F32)
    ONESC = sbt("ONESC", [128, 1], F32)
    EPSC = sbt("EPSC", [1, 1], F32)
    ONESR = sbt("ONESR", [1, 128], F32)
    ONESRB = sbt("ONESRB", [1, 128], BF16)
    GBR = sbt("GBR", [1, E], F32)
    GBF = sbt("GBF", [128, E], F32)
    LNBBF = sbt("LNBBF", [128, 8], BF16)
    XSQ = [sbt(f"XSQ{j}", [128, 512], F32) for j in range(2)]
    TMPB = [sbt(f"TMP{j}", [128, 512], F32) for j in range(2)]
    ROWT = sbt("ROWT", [128, 512], F32)
    R_M = ROWT[0:1, :]
    R_VAR = TMPB[0][0:1, :]
    R_L1 = TMPB[1][0:1, :]
    R_T = XSQ[0][0:1, :]
    R_RC = XSQ[1][0:1, :]

    SROW = UNION[0:1, 0:2048]
    BROW = UNION[0:1, 2048:4096]
    RHOROW = UNION[0:1, 4096:6144]
    GSB = [XZ_ALL[:, 24576 + 512 * j:24576 + 512 * (j + 1)] for j in range(2)]
    TBF = [XZ_ALL[:, 25600 + 512 * j:25600 + 512 * (j + 1)] for j in range(2)]
    PRB = [XZ_ALL[:, 26624 + 512 * j:26624 + 512 * (j + 1)] for j in range(4)]
    BCP = [XZ_ALL[:, 28672 + 512 * j:28672 + 512 * (j + 1)] for j in range(2)]

    BK = [e_(nc.psum_tensor(f"BK{b}", [128, 512], F32)) for b in range(8)]

    semnames = ["sPE", "sV", "sS", "sG", "sX", "sPar", "sParG",
                "sHin0", "sHin1", "sSl0", "sSl1",
                "sW0", "sW1", "sW2", "sW3", "sW4", "sW5",
                "sAGi", "sAG", "sAGo", "sStore"]
    semh = {n: e_(nc.semaphore(n)) for n in semnames}
    p.semh = semh
    for n in semnames:
        p.semv[n] = 0

    def mm(out, lhsT, rhs, start, stop, banks, inc="stop"):
        if start:
            p.bank_write(banks)
        do_inc = (inc == "always") or (inc == "stop" and stop)
        v = p.op("tensor", lambda h, o=out, l=lhsT, r=rhs, st=start, sp=stop:
                 h.matmul(o, lhsT=l, rhs=r, start=st, stop=sp),
                 inc=("sPE" if do_inc else None))
        if stop and do_inc:
            p.bank_produced(banks, v)
        return v

    def transpose(out, in_, banks):
        p.bank_write(banks)
        kk = in_.shape[0]
        v = p.op("tensor", lambda h, o=out, i=in_, k=kk: h.transpose(
            o, i, IDENT[0:k, 0:k]), inc="sPE")
        p.bank_produced(banks, v)
        return v

    def vop(makefn, read_banks=()):
        if read_banks:
            p.bank_read("vector", read_banks, "sV")
        return p.op("vector", makefn, inc="sV")

    def sop(makefn, read_banks=()):
        if read_banks:
            p.bank_read("scalar", read_banks, "sS")
        return p.op("scalar", makefn, inc="sS")

    def gop(makefn):
        return p.op("gpsimd", makefn, inc="sG")

    def dma(eng, out, in_, sem):
        return p.op(eng, lambda h, o=out, i=in_: h.dma_start(out=o, in_=i),
                    inc=sem, n=16)

    def tap(name, ap, shape, dt):
        if name not in taps:
            return
        t_ = nc.dram_tensor("tap_" + name, list(shape), dt, kind="ExternalOutput").ap()
        tapmeta[name] = True
        for s in ("sV", "sS", "sPE", "sG"):
            p.wait("sync", s, p.val(s))
        if ap.dtype != dt:
            p.wait("gpsimd", "sV", p.val("sV"))
            p.wait("gpsimd", "sS", p.val("sS"))
            dma("gpsimd", t_, ap, "sStore")
        else:
            dma("sync", t_, ap, "sStore")

    def finish():
        for s in ("sV", "sS", "sPE", "sG"):
            p.wait("sync", s, p.val(s))
        for k in range(KT):
            dma("sync", y_out[128 * k:128 * (k + 1), :], Xt(k), "sStore")
        p.wait("sync", "sStore", p.val("sStore"))
        with nc.Block() as block:
            for eng in ENGINES:
                dec = getattr(block, eng)

                def make_body(eng=eng):
                    def body(h):
                        for fn in p.stream[eng]:
                            fn(h)
                    return body
                dec(make_body())
        ctx.close()
        return nc, tapmeta

    for k in range(KT):
        dma("sync", Xt(k), x_in[128 * k:128 * (k + 1), :], "sX")
    dma("sync", PAR["ABAR"][:], A_all_in, "sPar")
    dma("sync", PAR["CBDT"][:], B_all_in, "sPar")
    dma("sync", PAR["LOGAB"][:], C_all_in, "sPar")
    dma("sync", PAR["LDT"][:], LDT_all_in, "sPar")
    dma("sync", INB[:], in_b_c, "sPar")
    dma("sync", IDENT[:], ident_in, "sPar")
    dma("sync", BEYE[:], beye_in, "sPar")
    dma("sync", GBR[:], gate_b_r, "sPar")
    dma("sync", DTCOL[0:1, 0:1], logdt_b, "sPar")
    for nm, src in [("OUTB", out_b_c), ("LNG", ln_g_c), ("LNB", ln_b_c),
                    ("LOGABB", logA_b_c), ("BBDT", bB_c), ("BCD", bC_c),
                    ("BDC2", bD_c), ("EV", e_vec), ("MV", m_vec)]:
        dma("sync", SM[nm][:], src, "sPar")
    PAR_TOT = p.val("sPar")
    dma("gpsimd", GW[:].rearrange("pk (a e) -> pk a e", e=E),
        gate_wT.rearrange("(a pk) e -> pk a e", pk=128), "sParG")
    dma("gpsimd", LNBBF[:], ln_b_c, "sParG")
    PARG_TOT = p.val("sParG")

    gop(lambda h: h.memset(ONESC[:], 1.0))
    gop(lambda h: h.memset(EPSC[:], EPS))
    gop(lambda h: h.memset(ONESR[:], 1.0))
    gop(lambda h: h.memset(ONESRB[:], 1.0))
    gop(lambda h: h.memset(MB[0][:], 0.0))
    gop(lambda h: h.memset(MB[1][:], 0.0))
    gop(lambda h: h.memset(PAR["HA"][:], 0.0))
    gop(lambda h: h.memset(SM["HBB"][:], 0.0))
    G_SETUP = p.val("sG")

    p.wait("scalar", "sPar", PAR_TOT)
    p.wait("vector", "sPar", PAR_TOT)
    sop(lambda h: h.activation(out=PAR["SCR0"][:], in_=PAR["LDT"][:], func=AF.Exp))
    p.wait("vector", "sS", p.val("sS"))
    vop(lambda h: h.tensor_scalar(out=PAR["SCR0"][:], in0=PAR["SCR0"][:],
                                  scalar1=1e-4, scalar2=1.0, op0=ALU.max, op1=ALU.min))
    p.wait("vector", "sV", p.val("sV"))
    vop(lambda h: h.tensor_tensor(out=PAR["CBDT"][:], in0=PAR["CBDT"][:],
                                  in1=PAR["LOGAB"][:], op=ALU.mult))
    p.wait("vector", "sV", p.val("sV"))
    vop(lambda h: h.tensor_tensor(out=PAR["CBDT"][:], in0=PAR["CBDT"][:],
                                  in1=PAR["SCR0"][:], op=ALU.mult))
    sop(lambda h: h.activation(out=PAR["SCR1"][:], in_=PAR["ABAR"][:], func=AF.Exp))
    p.wait("vector", "sS", p.val("sS"))
    vop(lambda h: h.tensor_tensor(out=PAR["SCR1"][:], in0=PAR["SCR1"][:],
                                  in1=PAR["SCR0"][:], op=ALU.mult))
    p.wait("vector", "sV", p.val("sV"))
    vop(lambda h: h.tensor_scalar(out=PAR["SCR1"][:], in0=PAR["SCR1"][:],
                                  scalar1=-1.0, op0=ALU.mult, scalar2=None))
    p.wait("vector", "sV", p.val("sV"))
    vop(lambda h: h.tensor_scalar(out=PAR["LOGAB"][:], in0=PAR["SCR1"][:],
                                  scalar1=LOG_LO, scalar2=LOG_HI,
                                  op0=ALU.max, op1=ALU.min))
    p.wait("scalar", "sV", p.val("sV"))
    sop(lambda h: h.activation(out=PAR["ABAR"][:], in_=PAR["LOGAB"][:], func=AF.Exp))
    sop(lambda h: h.activation(out=DTCOL[0:1, 0:1], in_=DTCOL[0:1, 0:1], func=AF.Exp))
    p.wait("tensor", "sS", p.val("sS"))
    p.wait("tensor", "sG", G_SETUP)
    mm(BK[0][:, 0:1], ONESR[:], DTCOL[0:1, 0:1], True, True, [0])
    vop(lambda h: h.tensor_copy(DTCOL[:], BK[0][:, 0:1]), read_banks=[0])
    sop(lambda h: h.activation(out=SM["ABARB"][:], in_=SM["LOGABB"][:], func=AF.Exp))
    p.wait("vector", "sS", p.val("sS"))
    vop(lambda h: h.tensor_scalar(out=SM["LOGABB"][:], in0=SM["ABARB"][:],
                                  scalar1=DTCOL[:, 0:1], op0=ALU.mult,
                                  scalar2=-1.0, op1=ALU.mult))
    p.wait("scalar", "sV", p.val("sV"))
    sop(lambda h: h.activation(out=SM["ABARB"][:], in_=SM["LOGABB"][:], func=AF.Exp))
    vop(lambda h: h.tensor_scalar(out=SM["BBDT"][:], in0=SM["BBDT"][:],
                                  scalar1=DTCOL[:, 0:1], op0=ALU.mult, scalar2=None))
    vop(lambda h: h.tensor_tensor(out=SM["BCD"][:], in0=SM["BCD"][:],
                                  in1=SM["BDC2"][:], op=ALU.add))
    mm(BK[1][:, 0:E], ONESR[:], GBR[:], True, True, [1])
    vop(lambda h: h.tensor_copy(GBF[:], BK[1][:, 0:E]), read_banks=[1])

    def ln_phase(kind):
        p.wait("scalar", "sG", G_SETUP)
        p.wait("scalar", "sX", 16 * KT)
        p.wait("tensor", "sX", 16 * KT)
        p.wait("scalar", "sV", p.val("sV"))
        p.wait("tensor", "sV", p.val("sV"))
        for tc in range(4):
            cs = slice(512 * tc, 512 * (tc + 1))
            for k in range(KT):
                buf = XSQ[k % 2]
                sv = sop(lambda h, b=buf, k=k, c=cs: h.activation(
                    out=b[:], in_=Xt(k)[:, c], func=AF.Square))
                p.wait("tensor", "sS", sv)
                mm(BK[0][0:1, :], ONESC[:], Xt(k)[:, cs], k == 0, k == KT - 1, [0],
                   inc="stop")
                vmm = mm(BK[1][0:1, :], ONESC[:], buf[:], k == 0, k == KT - 1, [1],
                         inc="always")
                if k + 2 < KT:
                    p.wait("scalar", "sPE", vmm)
            vop(lambda h: h.tensor_scalar(out=R_M, in0=BK[0][0:1, :],
                                          scalar1=1.0 / D, op0=ALU.mult, scalar2=None),
                read_banks=[0])
            vop(lambda h: h.tensor_scalar(out=R_T, in0=BK[1][0:1, :],
                                          scalar1=1.0 / D, op0=ALU.mult, scalar2=None),
                read_banks=[1])
            p.wait("vector", "sV", p.val("sV"))
            vop(lambda h: h.tensor_tensor(out=R_RC, in0=R_M, in1=R_M,
                                          op=ALU.mult))
            p.wait("vector", "sV", p.val("sV"))
            vop(lambda h: h.tensor_tensor(out=R_VAR, in0=R_T, in1=R_RC,
                                          op=ALU.subtract))
            p.wait("scalar", "sV", p.val("sV"))
            sop(lambda h: h.activation(out=R_L1, in_=R_VAR, func=AF.Ln,
                                       bias=EPSC[0:1, 0:1]))
            if kind == "ln1":
                vop(lambda h: h.tensor_scalar(out=R_T, in0=R_VAR, scalar1=EPS,
                                              op0=ALU.add, scalar2=None))
                p.wait("vector", "sV", p.val("sV"))
                vop(lambda h: h.reciprocal(R_RC, R_T))
                p.wait("vector", "sV", p.val("sV"))
                vop(lambda h: h.tensor_tensor(out=R_VAR, in0=R_VAR, in1=R_RC,
                                              op=ALU.mult))
                p.wait("scalar", "sV", p.val("sV"))
                sop(lambda h: h.activation(out=R_T, in_=R_VAR, func=AF.Ln,
                                           bias=EPSC[0:1, 0:1]))
                p.wait("vector", "sS", p.val("sS"))
                vop(lambda h: h.tensor_tensor(out=R_RC, in0=R_L1, in1=R_T,
                                              op=ALU.add))
                p.wait("scalar", "sV", p.val("sV"))
                sop(lambda h, c=cs: h.activation(out=SROW[:, c], in_=R_RC,
                                                 func=AF.Exp, scale=-0.5))
                sop(lambda h, c=cs: h.activation(out=RHOROW[:, c], in_=R_T,
                                                 func=AF.Exp, scale=0.5))
            else:
                sop(lambda h, c=cs: h.activation(out=SROW[:, c], in_=R_L1,
                                                 func=AF.Exp, scale=-0.5))
            p.wait("vector", "sS", p.val("sS"))
            vop(lambda h, c=cs: h.tensor_tensor(out=BROW[:, c], in0=R_M,
                                                in1=SROW[:, c], op=ALU.mult))
            p.wait("vector", "sV", p.val("sV"))
            vop(lambda h, c=cs: h.tensor_scalar(out=BROW[:, c], in0=BROW[:, c],
                                                scalar1=-1.0, op0=ALU.mult,
                                                scalar2=None))
            ba, bb = (2, 3) if tc % 2 == 0 else (4, 5)
            p.wait("tensor", "sV", p.val("sV"))
            mm(BK[ba][:], ONESR[:], SROW[0:1, cs], True, True, [ba])
            mm(BK[bb][:], ONESR[:], BROW[0:1, cs], True, True, [bb])
            for k in range(KT):
                vop(lambda h, k=k, c=cs, b=ba: h.tensor_tensor(
                    out=TMPB[k % 2][:], in0=Xt(k)[:, c], in1=BK[b][:], op=ALU.mult),
                    read_banks=[ba])
                p.wait("vector", "sV", p.val("sV"))
                vop(lambda h, k=k, c=cs, b=bb: h.tensor_tensor(
                    out=XNt(k)[:, c], in0=TMPB[k % 2][:], in1=BK[b][:], op=ALU.add),
                    read_banks=[bb])

    ln_phase("ln1")
    tap("xn1", XN_ALL[:, 0:T], [128, T], F32)
    if not stage_on("inproj"):
        return finish()

    def load_wslice(src2d, m, slot, sem):
        ap = src2d[:, 128 * m:128 * (m + 1)].rearrange("(kt pk) m -> pk kt m",
                                                       pk=128)
        dst = WSL[slot][:].rearrange("pk (kt m) -> pk kt m", m=128)
        dma("gpsimd", dst, ap, sem)
        return p.val(sem)

    slot_pe = [0, 0]
    LN1_V = p.val("sV")
    p.wait("tensor", "sV", LN1_V)

    def inproj_like(src2d, dst_t, bias_mode, mrange):
        for m in mrange:
            slot = m % 2
            ssem = "sSl0" if slot == 0 else "sSl1"
            p.wait("gpsimd", "sPE", slot_pe[slot])
            dval = load_wslice(src2d, m, slot, ssem)
            if bias_mode == "bias2silu":
                bb = 4 + (m % 2)
                p.wait("tensor", ssem, dval)
                if m == 0:
                    p.wait("tensor", "sParG", PARG_TOT)
                for k in range(KT):
                    mm(BK[bb][:, m:m + 1], WSL[slot][:, 128 * k:128 * (k + 1)],
                       LNBBF[:, k:k + 1], k == 0, k == KT - 1, [bb])
                sop(lambda h, m=m, b=bb: h.activation(
                    out=BIAS2[:, m:m + 1], in_=BK[b][:, m:m + 1], func=AF.Identity,
                    bias=INB[:, m:m + 1], scale=1.0), read_banks=[bb])
                p.wait("gpsimd", ssem, dval)
                p.wait("gpsimd", "sPE", p.val("sPE"))
                for k in range(KT):
                    gop(lambda h, s=slot, k=k: h.tensor_scalar(
                        out=WSL[s][:, 128 * k:128 * (k + 1)],
                        in0=WSL[s][:, 128 * k:128 * (k + 1)],
                        scalar1=SM["LNG"][:, k:k + 1], op0=ALU.mult, scalar2=None))
                p.wait("tensor", "sG", p.val("sG"))
            else:
                p.wait("tensor", ssem, dval)
            for tc in range(4):
                bk = 6 if (m * 4 + tc) % 2 == 0 else 7
                cs = slice(512 * tc, 512 * (tc + 1))
                for k in range(KT):
                    mm(BK[bk][:], WSL[slot][:, 128 * k:128 * (k + 1)],
                       XNt(k)[:, cs], k == 0, k == KT - 1, [bk])
                if bias_mode == "bias2silu":
                    fn = AF.Identity if m < 8 else AF.Silu
                    sop(lambda h, m=m, c=cs, b=bk, f=fn: h.activation(
                        out=dst_t(m)[:, c], in_=BK[b][:], func=f,
                        bias=BIAS2[:, m:m + 1], scale=1.0), read_banks=[bk])
                else:
                    sop(lambda h, m=m, c=cs, b=bk: h.activation(
                        out=dst_t(m)[:, c], in_=BK[b][:], func=AF.Identity,
                        bias=SM["BBDT"][:, m:m + 1], scale=DTCOL[:, 0:1]),
                        read_banks=[bk])
            slot_pe[slot] = p.val("sPE")

    inproj_like(in_wT, XZt, "bias2silu", range(16))
    XP_DONE_S = p.val("sS")
    tap("xp0", XZ_ALL[:, 0:T], [128, T], F32)
    tap("z0", XZ_ALL[:, 8 * T:9 * T], [128, T], F32)
    if not stage_on("scan1"):
        return finish()

    def expand_dma(t, hb):
        mt, s = t // 16, t % 16
        src = XZt(mt)[8 * s:8 * s + 8, :].unsqueeze(1).broadcast_to([8, 16, T])
        sem = "sHin0" if hb == 0 else "sHin1"
        dma("sync", HBt(hb), src, sem)
        return sem, p.val(sem)

    hb_scan_done = [0, 0]
    ho_cons = [0, 0]
    p.wait("sync", "sS", XP_DONE_S)
    for t in range(128):
        hb = t % 2
        p.wait("sync", "sV", hb_scan_done[hb])
        sem, dv = expand_dma(t, hb)
        p.wait("vector", sem, dv)
        p.wait("vector", "sV", ho_cons[hb])
        vop(lambda h, t=t, s=hb: h.tensor_tensor_scan(
            out=HBt(2 + s), data0=PAR["ABAR"][:, t:t + 1].broadcast_to([128, T]),
            data1=HBt(s), initial=0.0, op0=ALU.mult, op1=ALU.add))
        hb_scan_done[hb] = p.val("sV")
        p.wait("vector", "sV", hb_scan_done[hb])
        vop(lambda h, t=t, s=hb: h.tensor_copy(PAR["SSELF"][:, t:t + 1],
                                               HBt(2 + s)[:, T - 1:T]))
        ho_cons[hb] = p.val("sV")
        if t == 5:
            tap("hexp5", HBt(hb), [128, T], F32)
            tap("hscan5", HBt(2 + hb), [128, T], F32)
    tap("sself", PAR["SSELF"][:], [128, 128], F32)
    if not stage_on("carry"):
        return finish()

    def allgather_carry(ssrc, ag_in, ag_out, width, logab_ap, hdst, sjbufs):
        p.wait("sync", "sV", p.val("sV"))
        dma("sync", ag_in.ap(), ssrc, "sAGi")
        p.wait("gpsimd", "sAGi", p.val("sAGi"))
        p.op("gpsimd", lambda h, ai=ag_in, ao=ag_out: h.collective_compute(
            "AllGather", ALU.bypass, replica_groups=[list(range(NC))],
            ins=[ai.ap().opt()], outs=[ao.ap().opt()]), inc="sAG")
        agv = p.val("sAG")
        p.wait("sync", "sAG", agv)
        sj_cons = [0, 0]
        for j in range(7):
            buf = sjbufs[j % 2]
            p.wait("sync", "sV", sj_cons[j % 2])
            dma("sync", buf, ag_out.ap()[128 * j:128 * (j + 1), :], "sAGo")
            do = p.val("sAGo")
            scr = PAR["SCR0"][:, 0:width]
            scr1 = PAR["SCR1"][:, 0:width]
            vop(lambda h, la=logab_ap, s=scr, j=j: h.tensor_scalar(
                out=s, in0=la, scalar1=SM["EV"][:, j:j + 1], op0=ALU.mult,
                scalar2=None))
            p.wait("scalar", "sV", p.val("sV"))
            sop(lambda h, s=scr: h.activation(out=s, in_=s, func=AF.Exp, scale=2048.0))
            p.wait("vector", "sS", p.val("sS"))
            p.wait("vector", "sAGo", do)
            vop(lambda h, s=scr, s1=scr1, b=buf: h.tensor_tensor(
                out=s1, in0=s, in1=b, op=ALU.mult))
            sj_cons[j % 2] = p.val("sV")
            p.wait("vector", "sV", p.val("sV"))
            vop(lambda h, s1=scr1, j=j, hd=hdst: h.scalar_tensor_tensor(
                out=hd, in0=s1, scalar=SM["MV"][:, j:j + 1], in1=hd,
                op0=ALU.mult, op1=ALU.add))

    allgather_carry(PAR["SSELF"][:], ag_a_in, ag_a_out, 128, PAR["LOGAB"][:],
                    PAR["HA"][:], [PAR["SJ0"][:], PAR["SJ1"][:]])
    tap("ha", PAR["HA"][:], [128, 128], F32)
    if not stage_on("scan2"):
        return finish()

    hb_scan_done = [0, 0]
    fold_done = [0, 0]
    mb_used = [0, 0]
    for P_ in range(8):
        ybanks = [0, 1, 2, 3] if P_ % 2 == 0 else [4, 5, 6, 7]
        for g in range(16):
            t = 16 * P_ + g
            hb = t % 2
            p.wait("sync", "sV", hb_scan_done[hb])
            sem, dv = expand_dma(t, hb)
            mbuf = MB[t % 2]
            p.wait("gpsimd", "sPE", mb_used[t % 2])
            if t >= 2:
                gprev = (t - 2) % 16
                gop(lambda h, mb=mbuf, gp=gprev: h.memset(
                    mb[:, 8 * gp:8 * gp + 8], 0.0))
            gop(lambda h, mb=mbuf, g=g, t=t: h.tensor_scalar(
                out=mb[:, 8 * g:8 * g + 8], in0=BEYE[:],
                scalar1=PAR["CBDT"][:, t:t + 1], op0=ALU.mult, scalar2=None))
            gv = p.val("sG")
            p.wait("vector", sem, dv)
            p.wait("vector", "sPE", fold_done[hb])
            vop(lambda h, t=t, s=hb: h.tensor_tensor_scan(
                out=HBt(2 + s), data0=PAR["ABAR"][:, t:t + 1].broadcast_to([128, T]),
                data1=HBt(s), initial=PAR["HA"][:, t:t + 1],
                op0=ALU.mult, op1=ALU.add))
            hb_scan_done[hb] = p.val("sV")
            scan_v = hb_scan_done[hb]
            p.wait("tensor", "sG", gv)
            p.wait("tensor", "sV", scan_v)
            for q in range(4):
                mm(BK[ybanks[q]][:], mbuf[:], HBt(2 + hb)[:, 512 * q:512 * (q + 1)],
                   g == 0, g == 15, [ybanks[q]], inc="always")
            fold_done[hb] = p.val("sPE")
            mb_used[t % 2] = p.val("sPE")
        for q in range(4):
            vop(lambda h, P_=P_, q=q, b=ybanks[q]: h.tensor_copy(
                XZt(P_)[:, 512 * q:512 * (q + 1)], BK[b][:]), read_banks=[ybanks[q]])
    tap("y0", XZ_ALL[:, 0:T], [128, T], F32)
    if not stage_on("attn_a"):
        return finish()

    p.wait("vector", "sS", XP_DONE_S)
    for k in range(KT):
        vop(lambda h, k=k: h.tensor_tensor(out=XZt(k), in0=XZt(k), in1=XZt(8 + k),
                                           op=ALU.mult))
    PROD_V = p.val("sV")
    p.wait("tensor", "sV", PROD_V)
    for tc in range(4):
        mm(BK[tc][:], ONESR[:], RHOROW[0:1, 512 * tc:512 * (tc + 1)], True, True,
           [tc])
    for m in range(KT):
        slot = m % 2
        ssem = "sSl0" if slot == 0 else "sSl1"
        p.wait("gpsimd", "sPE", slot_pe[slot])
        dval = load_wslice(out_wT, m, slot, ssem)
        p.wait("tensor", ssem, dval)
        for tc in range(4):
            bk = 5 if (m * 4 + tc) % 2 == 0 else 7
            cs = slice(512 * tc, 512 * (tc + 1))
            for k in range(KT):
                mm(BK[bk][:], WSL[slot][:, 128 * k:128 * (k + 1)],
                   XZt(k)[:, cs], k == 0, k == KT - 1, [bk])
            vop(lambda h, m=m, c=cs, tc=tc: h.tensor_tensor(
                out=TMPB[0][:], in0=XNt(m)[:, c], in1=BK[tc][:], op=ALU.mult),
                read_banks=[tc])
            p.wait("vector", "sV", p.val("sV"))
            vop(lambda h, m=m, b=bk: h.scalar_tensor_tensor(
                out=TMPB[1][:], in0=BK[b][:], scalar=SM["OUTB"][:, m:m + 1],
                in1=TMPB[0][:], op0=ALU.add, op1=ALU.add), read_banks=[bk])
            p.wait("vector", "sV", p.val("sV"))
            vop(lambda h: h.tensor_scalar(out=TMPB[1][:], in0=TMPB[1][:],
                                          scalar1=10.0, scalar2=-10.0,
                                          op0=ALU.min, op1=ALU.max))
            p.wait("vector", "sV", p.val("sV"))
            vop(lambda h, m=m, c=cs: h.tensor_tensor(
                out=Xt(m)[:, c], in0=Xt(m)[:, c], in1=TMPB[1][:], op=ALU.add))
        slot_pe[slot] = p.val("sPE")
    tap("x_attn_a", X_ALL[:, 0:T], [128, T], F32)
    if not stage_on("ln2"):
        return finish()

    def vopd(makefn, read_banks=()):
        p.wait("vector", "sV", p.val("sV"))
        return vop(makefn, read_banks)

    def moe_block(callidx):
        MOE_ENTRY_PE = p.val("sPE")
        MOE_ENTRY_V = p.val("sV")
        p.wait("tensor", "sV", p.val("sV"))
        for tc4 in range(4):
            cs = slice(512 * tc4, 512 * (tc4 + 1))
            bk = 0 if tc4 % 2 == 0 else 1
            if callidx == 0 and tc4 == 0:
                p.wait("tensor", "sParG", PARG_TOT)
            for k in range(KT):
                mm(BK[bk][0:32, :], GW[:, 32 * k:32 * (k + 1)], XNt(k)[:, cs],
                   k == 0, k == KT - 1, [bk])
            vopd(lambda h, c=cs, b=bk: h.tensor_copy(LOGFM[:, c], BK[b][0:32, :]),
                read_banks=[bk])
        p.wait("tensor", "sV", p.val("sV"))
        for j in range(16):
            bk = 2 if j % 2 == 0 else 3
            transpose(BK[bk][0:128, 0:32], LOGFM[:, 128 * j:128 * (j + 1)], [bk])
            sop(lambda h, j=j, b=bk: h.activation(
                out=SCORES[:, 32 * j:32 * (j + 1)], in_=BK[b][0:128, 0:32],
                func=AF.Sigmoid), read_banks=[bk])
        p.wait("vector", "sS", p.val("sS"))
        vopd(lambda h: h.tensor_tensor(
            out=SCALL.rearrange("p (a b) -> p a b", b=32),
            in0=SCORES.rearrange("p (a b) -> p a b", b=32),
            in1=GBF[:].unsqueeze(1).broadcast_to([128, 16, 32]), op=ALU.add))
        s4 = SCALL.rearrange("p (a g r) -> p a g r", g=8, r=4)
        pairs = [(0, 1), (0, 2), (0, 3), (1, 2), (1, 3), (2, 3)]
        for idx, (a, b) in enumerate(pairs):
            if idx == 0:
                vopd(lambda h, a=a, b=b: h.tensor_tensor(
                    out=GSALL.rearrange("p (x g) -> p x g", g=8),
                    in0=s4[:, :, :, a], in1=s4[:, :, :, b], op=ALU.add))
            else:
                vopd(lambda h, a=a, b=b: h.tensor_tensor(
                    out=IND.rearrange("p (x g) -> p x g", g=8),
                    in0=s4[:, :, :, a], in1=s4[:, :, :, b], op=ALU.add))
                p.wait("vector", "sV", p.val("sV"))
                vopd(lambda h: h.tensor_tensor(out=GSALL[:], in0=GSALL[:],
                                              in1=IND[:], op=ALU.max))
        for it in range(4):
            vopd(lambda h: h.tensor_reduce(
                M16[:], GSALL.rearrange("p (x g) -> p x g", g=8), axis=AX.X,
                op=ALU.max))
            p.wait("vector", "sV", p.val("sV"))
            vopd(lambda h: h.tensor_tensor(
                out=IND.rearrange("p (x g) -> p x g", g=8),
                in0=GSALL.rearrange("p (x g) -> p x g", g=8),
                in1=M16.unsqueeze(2).broadcast_to([128, 16, 8]), op=ALU.is_ge))
            p.wait("vector", "sV", p.val("sV"))
            if it == 0:
                vopd(lambda h: h.tensor_copy(GMASK[:], IND[:]))
            else:
                vopd(lambda h: h.tensor_tensor(out=GMASK[:], in0=GMASK[:],
                                              in1=IND[:], op=ALU.add))
            vopd(lambda h: h.scalar_tensor_tensor(
                out=GSALL[:], in0=IND[:], scalar=-1e30, in1=GSALL[:],
                op0=ALU.mult, op1=ALU.add))
        vopd(lambda h: h.tensor_tensor(
            out=MASKED.rearrange("p (x g r) -> p x g r", g=8, r=4),
            in0=s4,
            in1=GMASK.rearrange("p (x g) -> p x g", g=8).unsqueeze(3)
                .broadcast_to([128, 16, 8, 4]), op=ALU.mult))
        for it in range(8):
            vopd(lambda h: h.tensor_reduce(
                M16[:], MASKED.rearrange("p (x b) -> p x b", b=32), axis=AX.X,
                op=ALU.max))
            p.wait("vector", "sV", p.val("sV"))
            tgx = SEL if it == 0 else CWT
            vopd(lambda h, tg=tgx: h.tensor_tensor(
                out=tg.rearrange("p (x b) -> p x b", b=32),
                in0=MASKED.rearrange("p (x b) -> p x b", b=32),
                in1=M16.unsqueeze(2).broadcast_to([128, 16, 32]), op=ALU.is_ge))
            p.wait("vector", "sV", p.val("sV"))
            if it > 0:
                vopd(lambda h: h.tensor_tensor(out=SEL[:], in0=SEL[:], in1=CWT[:],
                                              op=ALU.add))
            vopd(lambda h, tg=tgx: h.scalar_tensor_tensor(
                out=MASKED[:], in0=tg[:], scalar=-1e30, in1=MASKED[:],
                op0=ALU.mult, op1=ALU.add))
        vopd(lambda h: h.tensor_tensor(out=CWT[:], in0=SCORES[:], in1=SEL[:],
                                      op=ALU.mult))
        vopd(lambda h: h.tensor_reduce(
            W16[:], CWT.rearrange("p (x b) -> p x b", b=32), axis=AX.X, op=ALU.add))
        vopd(lambda h: h.tensor_scalar(out=W16[:], in0=W16[:], scalar1=1e-20,
                                      op0=ALU.add, scalar2=None))
        p.wait("vector", "sV", p.val("sV"))
        vopd(lambda h: h.reciprocal(RC16[:], W16[:]))
        vopd(lambda h: h.tensor_scalar(out=RC16[:], in0=RC16[:], scalar1=2.5,
                                      op0=ALU.mult, scalar2=None))
        p.wait("vector", "sV", p.val("sV"))
        vopd(lambda h: h.tensor_tensor(
            out=CWT.rearrange("p (x b) -> p x b", b=32),
            in0=CWT.rearrange("p (x b) -> p x b", b=32),
            in1=RC16.unsqueeze(2).broadcast_to([128, 16, 32]), op=ALU.mult))
        p.wait("tensor", "sV", p.val("sV"))
        for j in range(16):
            bk = 2 if j % 2 == 0 else 3
            transpose(BK[bk][0:32, 0:128], CWT[:, 32 * j:32 * (j + 1)], [bk])
            vopd(lambda h, j=j, b=bk: h.tensor_copy(
                CFM[:, 128 * j:128 * (j + 1)], BK[b][0:32, 0:128]),
                read_banks=[bk])
        ROUTE_V = p.val("sV")
        tap(f"combine{callidx}", CFM, [32, T], F32)
        tap(f"scores{callidx}", SCORES, [128, 512], F32)
        tap(f"gmask{callidx}", GMASK, [128, 128], F32)
        tap(f"sel{callidx}", SEL, [128, 512], F32)
        tap(f"cwt{callidx}", CWT, [128, 512], F32)
        if callidx == 0 and not stage_on("ffn1"):
            return False
        p.wait("tensor", "sV", ROUTE_V)
        bcp_used = [ROUTE_V, ROUTE_V]
        gsb_used = [0, 0]
        prod_v = [0] * 4
        slot_free = [0] * 6
        wbase = {i: p.val(f"sW{i}") for i in range(6)}
        wring_pe = [max(MOE_ENTRY_PE, 1)] * 6
        wring_v = [MOE_ENTRY_V] * 6
        ldq = []
        for pr in range(16):
            ldq += [("g", 2 * pr), ("g", 2 * pr + 1), ("u", 2 * pr),
                    ("u", 2 * pr + 1), ("d", 2 * pr), ("d", 2 * pr + 1)]
        ldq += [("sg", 0), ("sg", 1), ("su", 0), ("su", 1), ("sd", 0), ("sd", 1)]
        nld = [0]

        def issue_load():
            if nld[0] >= len(ldq):
                return
            kind, idx = ldq[nld[0]]
            slotidx = nld[0] % 6
            sem = f"sW{slotidx}"
            p.wait("gpsimd", "sPE", wring_pe[slotidx])
            p.wait("gpsimd", "sV", wring_v[slotidx])
            if kind == "g":
                ap = wgT[idx].rearrange("(kt pk) i -> pk kt i", pk=128)
            elif kind == "u":
                ap = wuT[idx].rearrange("(kt pk) i -> pk kt i", pk=128)
            elif kind == "d":
                ap = wdnT[idx].rearrange("(it pk) m -> pk it m", pk=128)
            elif kind == "sg":
                ap = shgT[:, 512 * idx:512 * (idx + 1)].rearrange(
                    "(kt pk) i -> pk kt i", pk=128)
            elif kind == "su":
                ap = shuT[:, 512 * idx:512 * (idx + 1)].rearrange(
                    "(kt pk) i -> pk kt i", pk=128)
            else:
                ap = shdT[512 * idx:512 * (idx + 1), :].rearrange(
                    "(it pk) m -> pk it m", pk=128)
            nfree = ap.shape[1] * ap.shape[2]
            dst = WRslot(slotidx).rearrange("pk (a b) -> pk a b", a=ap.shape[1])
            dma("gpsimd", dst, ap, sem)
            nld[0] += 1

        for _ in range(6):
            issue_load()

        for pr in range(17):
            shared = (pr == 16)
            wv = {f"sW{i}": wbase[i] + 16 * (pr + 1) for i in range(6)}
            sl_g0, sl_g1, sl_u0, sl_u1, sl_d0, sl_d1 = range(6)
            for tc in range(8):
                c2 = slice(256 * tc, 256 * (tc + 1))
                bsem = "sHin0" if tc % 2 == 0 else "sHin1"
                if not shared:
                    p.wait("sync", "sV", bcp_used[tc % 2])
                    dma("sync", BCP[tc % 2][:, 0:256],
                        CFM[2 * pr:2 * pr + 1, c2].unsqueeze(1)
                        .broadcast_to([1, 128, 256]), bsem)
                    dma("sync", BCP[tc % 2][:, 256:512],
                        CFM[2 * pr + 1:2 * pr + 2, c2].unsqueeze(1)
                        .broadcast_to([1, 128, 256]), bsem)
                bcdv = p.val(bsem)
                for it in range(4):
                    bG = 0 if it % 2 == 0 else 2
                    bU = 1 if it % 2 == 0 else 3
                    p.wait("tensor", "sW0", wv["sW0"])
                    p.wait("tensor", "sW1", wv["sW1"])
                    p.bank_write([bG])
                    for k in range(KT):
                        mm(BK[bG][:, 0:256],
                           WRslot(sl_g0)[:, 512 * k + 128 * it:512 * k + 128 * (it + 1)],
                           XNt(k)[:, c2], k == 0, False, [bG], inc=None)
                    for k in range(KT):
                        vv = mm(BK[bG][:, 256:512],
                                WRslot(sl_g1)[:, 512 * k + 128 * it:512 * k + 128 * (it + 1)],
                                XNt(k)[:, c2], k == 0, k == KT - 1, [bG],
                                inc=("always" if k == KT - 1 else None))
                    p.bank_produced([bG], vv)
                    if tc == 7 and it == 3:
                        slot_free[sl_g0] = slot_free[sl_g1] = vv
                    p.wait("scalar", "sV", gsb_used[it % 2])
                    sop(lambda h, b=bG, pp=it % 2: h.activation(
                        out=GSB[pp][:], in_=BK[b][:], func=AF.Silu), read_banks=[bG])
                    gsv = p.val("sS")
                    p.wait("tensor", "sW2", wv["sW2"])
                    p.wait("tensor", "sW3", wv["sW3"])
                    p.bank_write([bU])
                    for k in range(KT):
                        mm(BK[bU][:, 0:256],
                           WRslot(sl_u0)[:, 512 * k + 128 * it:512 * k + 128 * (it + 1)],
                           XNt(k)[:, c2], k == 0, False, [bU], inc=None)
                    for k in range(KT):
                        vv = mm(BK[bU][:, 256:512],
                                WRslot(sl_u1)[:, 512 * k + 128 * it:512 * k + 128 * (it + 1)],
                                XNt(k)[:, c2], k == 0, k == KT - 1, [bU],
                                inc=("always" if k == KT - 1 else None))
                    p.bank_produced([bU], vv)
                    p.wait("vector", "sS", gsv)
                    vop(lambda h, b=bU, pp=it % 2: h.tensor_tensor(
                        out=TBF[pp][:], in0=GSB[pp][:], in1=BK[b][:], op=ALU.mult),
                        read_banks=[bU])
                    gsb_used[it % 2] = p.val("sV")
                    p.wait("vector", "sV", p.val("sV"))
                    if shared:
                        vop(lambda h, it=it, pp=it % 2: h.tensor_copy(
                            PRB[it][:], TBF[pp][:]))
                        prod_v[it] = p.val("sV")
                    else:
                        p.wait("vector", bsem, bcdv)
                        vop(lambda h, it=it, pp=it % 2, bp=tc % 2: h.tensor_tensor(
                            out=PRB[it][:], in0=TBF[pp][:], in1=BCP[bp][:],
                            op=ALU.mult))
                        bcp_used[tc % 2] = p.val("sV")
                    prod_v[it] = p.val("sV")
                PR_V = p.val("sV")
                p.wait("tensor", "sW4", wv["sW4"])
                p.wait("tensor", "sW5", wv["sW5"])
                p.bank_write([4, 5, 6, 7])
                last_v = 0
                for it in range(4):
                    p.wait("tensor", "sV", prod_v[it])
                    for half, sld in ((0, sl_d0), (1, sl_d1)):
                        for m in range(KT):
                            bD = 4 + m // 2
                            dsl = BK[bD][:, 256 * (m % 2):256 * (m % 2) + 256]
                            first = (it == 0 and half == 0 and m % 2 == 0)
                            last = (it == 3 and half == 1)
                            last_v = mm(dsl,
                                        WRslot(sld)[:, 1024 * it + 128 * m:
                                                    1024 * it + 128 * (m + 1)],
                                        PRB[it][:, 256 * half:256 * (half + 1)],
                                        first, last, [bD],
                                        inc=("always" if last else None))
                p.bank_produced([4, 5, 6, 7], last_v)
                if tc == 7:
                    slot_free[sl_d0] = slot_free[sl_d1] = last_v
                for bb_ in range(4):
                    m0 = 2 * bb_
                    xap = X_ALL[:, T * m0:T * (m0 + 2)].rearrange(
                        "p (m t) -> p m t", m=2)[:, :, 256 * tc:256 * (tc + 1)]
                    vop(lambda h, b=4 + bb_, xa=xap: h.tensor_tensor(
                        out=xa, in0=xa,
                        in1=BK[b][:].rearrange("p (m t) -> p m t", m=2),
                        op=ALU.add), read_banks=[4 + bb_])
            done_v = p.val("sV")
            for i in range(6):
                wring_pe[i] = slot_free[i]
                wring_v[i] = done_v
            for _ in range(6):
                issue_load()
        return True

    ln_phase("plain")
    tap("hn1", XN_ALL[:, 0:T], [128, T], F32)
    if not stage_on("routing1"):
        return finish()
    cont = moe_block(0)
    tap("x_ffn1", X_ALL[:, 0:T], [128, T], F32)
    if not cont or not stage_on("attn_b"):
        return finish()

    ln_phase("plain")
    inproj_like(wbT, XZt, "bu", range(8))
    BU_S = p.val("sS")
    p.wait("vector", "sS", BU_S)
    for k in range(KT):
        vop(lambda h, k=k: h.tensor_tensor_scan(
            out=HBt(2), data0=SM["ABARB"][:, k:k + 1].broadcast_to([128, T]),
            data1=XZt(k), initial=0.0, op0=ALU.mult, op1=ALU.add))
        p.wait("vector", "sV", p.val("sV"))
        vop(lambda h, k=k: h.tensor_copy(SM["SBB"][:, k:k + 1], HBt(2)[:, T - 1:T]))
    allgather_carry(SM["SBB"][:], ag_b_in, ag_b_out, 8, SM["LOGABB"][:],
                    SM["HBB"][:], [SM["SJB"][:], SM["SJB2"][:]])
    for k in range(KT):
        vop(lambda h, k=k: h.tensor_tensor_scan(
            out=XZt(8 + k), data0=SM["ABARB"][:, k:k + 1].broadcast_to([128, T]),
            data1=XZt(k), initial=SM["HBB"][:, k:k + 1], op0=ALU.mult, op1=ALU.add))
    XS_V = p.val("sV")
    tap("xs0", XZ_ALL[:, 8 * T:9 * T], [128, T], F32)
    p.wait("tensor", "sV", XS_V)
    for m in range(KT):
        p.wait("gpsimd", "sPE", slot_pe[0])
        dc = load_wslice(wcT, m, 0, "sSl0")
        p.wait("gpsimd", "sPE", slot_pe[1])
        dd = load_wslice(wdT, m, 1, "sSl1")
        p.wait("tensor", "sSl0", dc)
        p.wait("tensor", "sSl1", dd)
        for tc in range(4):
            bk = 6 if (m * 4 + tc) % 2 == 0 else 7
            cs = slice(512 * tc, 512 * (tc + 1))
            for k in range(KT):
                mm(BK[bk][:], WSL[0][:, 128 * k:128 * (k + 1)], XZt(8 + k)[:, cs],
                   k == 0, False, [bk], inc=None)
            for k in range(KT):
                vv = mm(BK[bk][:], WSL[1][:, 128 * k:128 * (k + 1)], XNt(k)[:, cs],
                        False, k == KT - 1, [bk],
                        inc=("always" if k == KT - 1 else None))
            p.bank_produced([bk], vv)
            vop(lambda h, m=m, c=cs, b=bk: h.scalar_tensor_tensor(
                out=Xt(m)[:, c], in0=BK[b][:], scalar=SM["BCD"][:, m:m + 1],
                in1=Xt(m)[:, c], op0=ALU.add, op1=ALU.add), read_banks=[bk])
        slot_pe[0] = p.val("sPE")
        slot_pe[1] = p.val("sPE")
    tap("x_attn_b", X_ALL[:, 0:T], [128, T], F32)
    if not stage_on("ffn2"):
        return finish()

    ln_phase("plain")
    moe_block(1)
    tap("x_ffn2", X_ALL[:, 0:T], [128, T], F32)
    return finish()


def _prep_inputs(inputs):
    f32 = np.float32

    def c(a):
        return np.ascontiguousarray(a, dtype=f32)

    x = np.asarray(inputs["L0_attn_a_x"], dtype=f32)[0]
    com = {
        "in_wT": c(np.asarray(inputs["L0_attn_a_in_w"], dtype=f32).T),
        "out_wT": c(np.asarray(inputs["L0_attn_a_out_w"], dtype=f32).T),
        "wbT": c(np.asarray(inputs["L0_attn_b_W_B"], dtype=f32).T),
        "wcT": c(np.asarray(inputs["L0_attn_b_W_C"], dtype=f32).T),
        "wdT": c(np.asarray(inputs["L0_attn_b_W_D"], dtype=f32).T),
        "gate_wT": c(np.asarray(inputs["L0_ffn_gate_w"], dtype=f32).T),
        "wgT": c(np.asarray(inputs["L0_ffn_w_gate"], dtype=f32).transpose(0, 2, 1)),
        "wuT": c(np.asarray(inputs["L0_ffn_w_up"], dtype=f32).transpose(0, 2, 1)),
        "wdnT": c(np.asarray(inputs["L0_ffn_w_down"], dtype=f32).transpose(0, 2, 1)),
        "shgT": c(np.asarray(inputs["L0_ffn_sh_gate"], dtype=f32).T),
        "shuT": c(np.asarray(inputs["L0_ffn_sh_up"], dtype=f32).T),
        "shdT": c(np.asarray(inputs["L0_ffn_sh_down"], dtype=f32).T),
        "A_all": c(np.asarray(inputs["L0_attn_a_A_log"], dtype=f32)
                   .reshape(128, 128).T),
        "B_all": c(np.asarray(inputs["L0_attn_a_B_param"], dtype=f32)
                   .reshape(128, 128).T),
        "C_all": c(np.asarray(inputs["L0_attn_a_C_param"], dtype=f32)
                   .reshape(128, 128).T),
        "ident_in": np.eye(128, dtype=f32),
        "beye_in": (np.arange(128)[:, None] // 16 ==
                    np.arange(8)[None, :]).astype(f32),
        "gate_b_r": c(np.asarray(inputs["L0_ffn_gate_bias"], dtype=f32)
                      .reshape(1, E)),
        "logdt_b": c(np.asarray(inputs["L0_attn_b_log_dt"], dtype=f32)
                     .reshape(1, 1)),
    }
    ld = np.asarray(inputs["L0_attn_a_log_dt"], dtype=f32)
    idx = (8 * np.arange(128))[None, :] + (np.arange(128) // 16)[:, None]
    com["LDT_all"] = c(ld[idx])

    def col128(v):
        return c(np.asarray(v, dtype=f32).reshape(-1, 128).T)

    com["in_b_c"] = col128(inputs["L0_attn_a_in_b"])
    com["out_b_c"] = col128(inputs["L0_attn_a_out_b"])
    com["ln_g_c"] = col128(inputs["L0_attn_a_ln_g"])
    com["ln_b_c"] = col128(inputs["L0_attn_a_ln_b"])
    com["logA_b_c"] = col128(inputs["L0_attn_b_log_A"])
    com["bB_c"] = col128(inputs["L0_attn_b_b_B"])
    com["bC_c"] = col128(inputs["L0_attn_b_b_C"])
    com["bD_c"] = col128(inputs["L0_attn_b_b_D"])

    in_maps = []
    for cix in range(NC):
        m = dict(com)
        m["x_in"] = c(x[T * cix:T * (cix + 1), :].T)
        ev = np.zeros((128, 8), dtype=f32)
        mv = np.zeros((128, 8), dtype=f32)
        for j in range(7):
            if j < cix:
                ev[:, j] = cix - 1 - j
                mv[:, j] = 1.0
        m["e_vec"] = ev
        m["m_vec"] = mv
        in_maps.append(m)
    return in_maps


def kernel(**inputs):
    in_maps = _prep_inputs(inputs)
    nc, _ = build("full")
    res = run_bass_kernel_spmd(nc, in_maps, core_ids=list(range(NC)))
    parts = [np.asarray(r["y_out"]).T for r in res.results]
    out = np.concatenate(parts, axis=0).reshape(1, NC * T, D)
    return np.ascontiguousarray(out, dtype=np.float32)


# version 04
# speedup vs baseline: 1.2615x; 1.0100x over previous
import numpy as np
import ml_dtypes
import concourse.bass as bass
import concourse.mybir as mybir
from concourse.bass_utils import run_bass_kernel_spmd

F32 = mybir.dt.float32
BF16 = mybir.dt.bfloat16
AF = mybir.ActivationFunctionType
ALU = mybir.AluOpType
AX = mybir.AxisListType

NC = 8
T = 2048
D = 1024
KT = 8
E = 32
EPS = 1e-5
LOG_LO = float(np.log(1e-8))
LOG_HI = float(np.log1p(-1e-8))

ENGINES = ("tensor", "vector", "scalar", "gpsimd", "sync")

STAGES = ["ln1", "inproj", "scan1", "carry", "scan2", "attn_a",
          "ln2", "routing1", "ffn1", "attn_b", "ffn2", "full"]


class Plan:
    def __init__(self):
        self.stream = {e: [] for e in ENGINES}
        self.semv = {}
        self.semh = {}
        self.bank_prod = [None] * 8
        self.bank_cons = [[] for _ in range(8)]

    def wait(self, eng, sem, val):
        if val <= 0:
            return
        self.stream[eng].append(lambda h, s=sem, v=val: h.wait_ge(self.semh[s], v))

    def op(self, eng, makefn, inc=None, n=1):
        if inc is not None:
            self.semv[inc] = self.semv.get(inc, 0) + n
            def fn(h, mk=makefn, s=inc, k=n):
                mk(h).then_inc(self.semh[s], k)
        else:
            def fn(h, mk=makefn):
                mk(h)
        self.stream[eng].append(fn)
        return self.semv.get(inc, 0)

    def val(self, sem):
        return self.semv.get(sem, 0)

    def bank_write(self, banks):
        for b in banks:
            for (s, v) in self.bank_cons[b]:
                self.wait("tensor", s, v)
            self.bank_cons[b] = []

    def bank_produced(self, banks, val):
        for b in banks:
            self.bank_prod[b] = ("sPE", val)

    def bank_read(self, eng, banks, marker_sem):
        for b in banks:
            if self.bank_prod[b] is not None:
                s, v = self.bank_prod[b]
                self.wait(eng, s, v)
        mv = self.val(marker_sem) + 1
        for b in banks:
            self.bank_cons[b].append((marker_sem, mv))


def build(debug_stage="full", taps=()):
    stage_idx = STAGES.index(debug_stage)

    def stage_on(name):
        return STAGES.index(name) <= stage_idx

    nc = bass.Bass("TRN2", target_bir_lowering=False, debug=False)
    p = Plan()
    taps = set(taps)
    tapmeta = {}

    def din(name, shape, dt=F32):
        return nc.dram_tensor(name, shape, dt, kind="ExternalInput").ap()

    x_in = din("x_in", [D, T])
    in_wT = din("in_wT", [D, 2 * D])
    out_wT = din("out_wT", [D, D])
    wbT = din("wbT", [D, D])
    wcT = din("wcT", [D, D])
    wdT = din("wdT", [D, D])
    gate_wT = din("gate_wT", [D, E])
    wgT = din("wgT", [E, D, 512])
    wuT = din("wuT", [E, D, 512])
    wdnT = din("wdnT", [E, 512, D])
    shgT = din("shgT", [D, D])
    shuT = din("shuT", [D, D])
    shdT = din("shdT", [D, D])
    A_all_in = din("A_all", [128, 128])
    B_all_in = din("B_all", [128, 128])
    C_all_in = din("C_all", [128, 128])
    LDT_all_in = din("LDT_all", [128, 128])
    in_b_c = din("in_b_c", [128, 16])
    out_b_c = din("out_b_c", [128, 8])
    ln_g_c = din("ln_g_c", [128, 8])
    ln_b_c = din("ln_b_c", [128, 8])
    logA_b_c = din("logA_b_c", [128, 8])
    logdt_b = din("logdt_b", [1, 1])
    bB_c = din("bB_c", [128, 8])
    bC_c = din("bC_c", [128, 8])
    bD_c = din("bD_c", [128, 8])
    gate_b_r = din("gate_b_r", [1, E])
    e_vec = din("e_vec", [128, 8])
    m_vec = din("m_vec", [128, 8])
    ident_in = din("ident_in", [128, 128])
    beye_in = din("beye_in", [128, 8])

    y_out = nc.dram_tensor("y_out", [D, T], F32, kind="ExternalOutput").ap()

    ag_a_in = nc.dram_tensor("ag_a_in", [128, 128], F32)
    ag_a_out = nc.dram_tensor("ag_a_out", [NC * 128, 128], F32)
    ag_b_in = nc.dram_tensor("ag_b_in", [128, 8], F32)
    ag_b_out = nc.dram_tensor("ag_b_out", [NC * 128, 8], F32)

    from contextlib import ExitStack
    ctx = ExitStack()
    e_ = ctx.enter_context

    def sbt(name, shape, dt):
        return e_(nc.sbuf_tensor(name, shape, dt))

    X_ALL = sbt("X_ALL", [128, KT * T], F32)
    XN_ALL = sbt("XN_ALL", [128, KT * T], BF16)
    XZ_ALL = sbt("XZ_ALL", [128, 16 * T], BF16)
    UNION = sbt("UNION", [128, 6144], F32)

    def Xt(k):
        return X_ALL[:, T * k:T * (k + 1)]

    def XNt(k):
        return XN_ALL[:, T * k:T * (k + 1)]

    def XZt(j):
        return XZ_ALL[:, T * j:T * (j + 1)]

    def WRslot(s):
        return XZ_ALL[:, 4096 * s:4096 * (s + 1)]

    UB = UNION.bitcast(BF16)

    def HBt(j):
        return UB[:, T * j:T * (j + 1)]

    LOGFM = UNION[0:32, 0:2048]
    SCORES = UNION[:, 2048:2560]
    SCALL = UNION[:, 2560:3072]
    MASKED = UNION[:, 3072:3584]
    SEL = UNION[:, 3584:4096]
    CWT = UNION[:, 4096:4608]
    GSALL = UNION[:, 4608:4736]
    GMASK = UNION[:, 4736:4864]
    IND = UNION[:, 4864:4992]
    M16 = UNION[:, 4992:5008]
    W16 = UNION[:, 5008:5024]
    RC16 = UNION[:, 5024:5040]
    CFM = XZ_ALL[0:32, 29696:31744]

    WSL = [sbt(f"WSL{j}", [128, 1024], BF16) for j in range(2)]
    GW = sbt("GW", [128, E * KT], BF16)
    PARN = ["ABAR", "CBDT", "LOGAB", "LDT", "SSELF", "HA", "SJ0", "SJ1",
            "SCR0", "SCR1"]
    PAR = {n: sbt(n, [128, 128], F32) for n in PARN}
    BEYE = sbt("BEYE", [128, 8], F32)
    MB = [sbt(f"MB{j}", [128, 128], BF16) for j in range(2)]
    SM_N = ["OUTB", "LNG", "LNB", "LOGABB", "ABARB", "BBDT", "BCD", "BDC2",
            "SBB", "HBB", "EV", "MV", "SJB", "SJB2"]
    SM = {n: sbt(n, [128, 8], F32) for n in SM_N}
    INB = sbt("INB", [128, 16], F32)
    BIAS2 = sbt("BIAS2", [128, 16], F32)
    DTCOL = sbt("DTCOL", [128, 1], F32)
    IDENT = sbt("IDENT", [128, 128], F32)
    ONESC = sbt("ONESC", [128, 1], F32)
    EPSC = sbt("EPSC", [1, 1], F32)
    ONESR = sbt("ONESR", [1, 128], F32)
    ONESRB = sbt("ONESRB", [1, 128], BF16)
    GBR = sbt("GBR", [1, E], F32)
    GBF = sbt("GBF", [128, E], F32)
    LNBBF = sbt("LNBBF", [128, 8], BF16)
    XSQ = [sbt(f"XSQ{j}", [128, 512], F32) for j in range(2)]
    TMPB = [sbt(f"TMP{j}", [128, 512], F32) for j in range(2)]
    ROWT = sbt("ROWT", [128, 512], F32)
    R_M = ROWT[0:1, :]
    R_VAR = TMPB[0][0:1, :]
    R_L1 = TMPB[1][0:1, :]
    R_T = XSQ[0][0:1, :]
    R_RC = XSQ[1][0:1, :]

    SROW = UNION[0:1, 0:2048]
    BROW = UNION[0:1, 2048:4096]
    RHOROW = UNION[0:1, 4096:6144]
    GSB = [XZ_ALL[:, 24576 + 512 * j:24576 + 512 * (j + 1)] for j in range(2)]
    TBF = [XZ_ALL[:, 25600 + 512 * j:25600 + 512 * (j + 1)] for j in range(2)]
    PRB = [XZ_ALL[:, 26624 + 512 * j:26624 + 512 * (j + 1)] for j in range(4)]
    BCP = [XZ_ALL[:, 28672 + 512 * j:28672 + 512 * (j + 1)] for j in range(2)]

    BK = [e_(nc.psum_tensor(f"BK{b}", [128, 512], F32)) for b in range(8)]

    semnames = ["sPE", "sV", "sS", "sG", "sX", "sPar", "sParG",
                "sHin0", "sHin1", "sSl0", "sSl1",
                "sW0", "sW1", "sW2", "sW3", "sW4", "sW5",
                "sAGi", "sAG", "sAGo", "sStore"]
    semh = {n: e_(nc.semaphore(n)) for n in semnames}
    p.semh = semh
    for n in semnames:
        p.semv[n] = 0

    def mm(out, lhsT, rhs, start, stop, banks, inc="stop"):
        if start:
            p.bank_write(banks)
        do_inc = (inc == "always") or (inc == "stop" and stop)
        v = p.op("tensor", lambda h, o=out, l=lhsT, r=rhs, st=start, sp=stop:
                 h.matmul(o, lhsT=l, rhs=r, start=st, stop=sp),
                 inc=("sPE" if do_inc else None))
        if stop and do_inc:
            p.bank_produced(banks, v)
        return v

    def transpose(out, in_, banks):
        p.bank_write(banks)
        kk = in_.shape[0]
        v = p.op("tensor", lambda h, o=out, i=in_, k=kk: h.transpose(
            o, i, IDENT[0:k, 0:k]), inc="sPE")
        p.bank_produced(banks, v)
        return v

    def vop(makefn, read_banks=()):
        if read_banks:
            p.bank_read("vector", read_banks, "sV")
        return p.op("vector", makefn, inc="sV")

    def sop(makefn, read_banks=()):
        if read_banks:
            p.bank_read("scalar", read_banks, "sS")
        return p.op("scalar", makefn, inc="sS")

    def gop(makefn):
        return p.op("gpsimd", makefn, inc="sG")

    def dma(eng, out, in_, sem):
        return p.op(eng, lambda h, o=out, i=in_: h.dma_start(out=o, in_=i),
                    inc=sem, n=16)

    def tap(name, ap, shape, dt):
        if name not in taps:
            return
        t_ = nc.dram_tensor("tap_" + name, list(shape), dt, kind="ExternalOutput").ap()
        tapmeta[name] = True
        for s in ("sV", "sS", "sPE", "sG"):
            p.wait("sync", s, p.val(s))
        if ap.dtype != dt:
            p.wait("gpsimd", "sV", p.val("sV"))
            p.wait("gpsimd", "sS", p.val("sS"))
            dma("gpsimd", t_, ap, "sStore")
        else:
            dma("sync", t_, ap, "sStore")

    def finish():
        for s in ("sV", "sS", "sPE", "sG"):
            p.wait("sync", s, p.val(s))
        for k in range(KT):
            dma("sync", y_out[128 * k:128 * (k + 1), :], Xt(k), "sStore")
        p.wait("sync", "sStore", p.val("sStore"))
        with nc.Block() as block:
            for eng in ENGINES:
                dec = getattr(block, eng)

                def make_body(eng=eng):
                    def body(h):
                        for fn in p.stream[eng]:
                            fn(h)
                    return body
                dec(make_body())
        ctx.close()
        return nc, tapmeta

    for k in range(KT):
        dma("sync", Xt(k), x_in[128 * k:128 * (k + 1), :], "sX")
    dma("sync", PAR["ABAR"][:], A_all_in, "sPar")
    dma("sync", PAR["CBDT"][:], B_all_in, "sPar")
    dma("sync", PAR["LOGAB"][:], C_all_in, "sPar")
    dma("sync", PAR["LDT"][:], LDT_all_in, "sPar")
    dma("sync", INB[:], in_b_c, "sPar")
    dma("sync", IDENT[:], ident_in, "sPar")
    dma("sync", BEYE[:], beye_in, "sPar")
    dma("sync", GBR[:], gate_b_r, "sPar")
    dma("sync", DTCOL[0:1, 0:1], logdt_b, "sPar")
    for nm, src in [("OUTB", out_b_c), ("LNG", ln_g_c), ("LNB", ln_b_c),
                    ("LOGABB", logA_b_c), ("BBDT", bB_c), ("BCD", bC_c),
                    ("BDC2", bD_c), ("EV", e_vec), ("MV", m_vec)]:
        dma("sync", SM[nm][:], src, "sPar")
    PAR_TOT = p.val("sPar")
    dma("gpsimd", GW[:].rearrange("pk (a e) -> pk a e", e=E),
        gate_wT.rearrange("(a pk) e -> pk a e", pk=128), "sParG")
    dma("gpsimd", LNBBF[:], ln_b_c, "sParG")
    PARG_TOT = p.val("sParG")

    gop(lambda h: h.memset(ONESC[:], 1.0))
    gop(lambda h: h.memset(EPSC[:], EPS))
    gop(lambda h: h.memset(ONESR[:], 1.0))
    gop(lambda h: h.memset(ONESRB[:], 1.0))
    gop(lambda h: h.memset(MB[0][:], 0.0))
    gop(lambda h: h.memset(MB[1][:], 0.0))
    gop(lambda h: h.memset(PAR["HA"][:], 0.0))
    gop(lambda h: h.memset(SM["HBB"][:], 0.0))
    G_SETUP = p.val("sG")

    p.wait("scalar", "sPar", PAR_TOT)
    p.wait("vector", "sPar", PAR_TOT)
    sop(lambda h: h.activation(out=PAR["SCR0"][:], in_=PAR["LDT"][:], func=AF.Exp))
    p.wait("vector", "sS", p.val("sS"))
    vop(lambda h: h.tensor_scalar(out=PAR["SCR0"][:], in0=PAR["SCR0"][:],
                                  scalar1=1e-4, scalar2=1.0, op0=ALU.max, op1=ALU.min))
    p.wait("vector", "sV", p.val("sV"))
    vop(lambda h: h.tensor_tensor(out=PAR["CBDT"][:], in0=PAR["CBDT"][:],
                                  in1=PAR["LOGAB"][:], op=ALU.mult))
    p.wait("vector", "sV", p.val("sV"))
    vop(lambda h: h.tensor_tensor(out=PAR["CBDT"][:], in0=PAR["CBDT"][:],
                                  in1=PAR["SCR0"][:], op=ALU.mult))
    sop(lambda h: h.activation(out=PAR["SCR1"][:], in_=PAR["ABAR"][:], func=AF.Exp))
    p.wait("vector", "sS", p.val("sS"))
    vop(lambda h: h.tensor_tensor(out=PAR["SCR1"][:], in0=PAR["SCR1"][:],
                                  in1=PAR["SCR0"][:], op=ALU.mult))
    p.wait("vector", "sV", p.val("sV"))
    vop(lambda h: h.tensor_scalar(out=PAR["SCR1"][:], in0=PAR["SCR1"][:],
                                  scalar1=-1.0, op0=ALU.mult, scalar2=None))
    p.wait("vector", "sV", p.val("sV"))
    vop(lambda h: h.tensor_scalar(out=PAR["LOGAB"][:], in0=PAR["SCR1"][:],
                                  scalar1=LOG_LO, scalar2=LOG_HI,
                                  op0=ALU.max, op1=ALU.min))
    p.wait("scalar", "sV", p.val("sV"))
    sop(lambda h: h.activation(out=PAR["ABAR"][:], in_=PAR["LOGAB"][:], func=AF.Exp))
    sop(lambda h: h.activation(out=DTCOL[0:1, 0:1], in_=DTCOL[0:1, 0:1], func=AF.Exp))
    p.wait("tensor", "sS", p.val("sS"))
    p.wait("tensor", "sG", G_SETUP)
    mm(BK[0][:, 0:1], ONESR[:], DTCOL[0:1, 0:1], True, True, [0])
    vop(lambda h: h.tensor_copy(DTCOL[:], BK[0][:, 0:1]), read_banks=[0])
    sop(lambda h: h.activation(out=SM["ABARB"][:], in_=SM["LOGABB"][:], func=AF.Exp))
    p.wait("vector", "sS", p.val("sS"))
    vop(lambda h: h.tensor_scalar(out=SM["LOGABB"][:], in0=SM["ABARB"][:],
                                  scalar1=DTCOL[:, 0:1], op0=ALU.mult,
                                  scalar2=-1.0, op1=ALU.mult))
    p.wait("scalar", "sV", p.val("sV"))
    sop(lambda h: h.activation(out=SM["ABARB"][:], in_=SM["LOGABB"][:], func=AF.Exp))
    vop(lambda h: h.tensor_scalar(out=SM["BBDT"][:], in0=SM["BBDT"][:],
                                  scalar1=DTCOL[:, 0:1], op0=ALU.mult, scalar2=None))
    vop(lambda h: h.tensor_tensor(out=SM["BCD"][:], in0=SM["BCD"][:],
                                  in1=SM["BDC2"][:], op=ALU.add))
    mm(BK[1][:, 0:E], ONESR[:], GBR[:], True, True, [1])
    vop(lambda h: h.tensor_copy(GBF[:], BK[1][:, 0:E]), read_banks=[1])

    def ln_phase(kind):
        p.wait("scalar", "sG", G_SETUP)
        p.wait("scalar", "sX", 16 * KT)
        p.wait("tensor", "sX", 16 * KT)
        p.wait("scalar", "sV", p.val("sV"))
        p.wait("tensor", "sV", p.val("sV"))
        for tc in range(4):
            cs = slice(512 * tc, 512 * (tc + 1))
            for k in range(KT):
                buf = XSQ[k % 2]
                sv = sop(lambda h, b=buf, k=k, c=cs: h.activation(
                    out=b[:], in_=Xt(k)[:, c], func=AF.Square))
                p.wait("tensor", "sS", sv)
                mm(BK[0][0:1, :], ONESC[:], Xt(k)[:, cs], k == 0, k == KT - 1, [0],
                   inc="stop")
                vmm = mm(BK[1][0:1, :], ONESC[:], buf[:], k == 0, k == KT - 1, [1],
                         inc="always")
                if k + 2 < KT:
                    p.wait("scalar", "sPE", vmm)
            vop(lambda h: h.tensor_scalar(out=R_M, in0=BK[0][0:1, :],
                                          scalar1=1.0 / D, op0=ALU.mult, scalar2=None),
                read_banks=[0])
            vop(lambda h: h.tensor_scalar(out=R_T, in0=BK[1][0:1, :],
                                          scalar1=1.0 / D, op0=ALU.mult, scalar2=None),
                read_banks=[1])
            p.wait("vector", "sV", p.val("sV"))
            vop(lambda h: h.tensor_tensor(out=R_RC, in0=R_M, in1=R_M,
                                          op=ALU.mult))
            p.wait("vector", "sV", p.val("sV"))
            vop(lambda h: h.tensor_tensor(out=R_VAR, in0=R_T, in1=R_RC,
                                          op=ALU.subtract))
            p.wait("scalar", "sV", p.val("sV"))
            sop(lambda h: h.activation(out=R_L1, in_=R_VAR, func=AF.Ln,
                                       bias=EPSC[0:1, 0:1]))
            if kind == "ln1":
                vop(lambda h: h.tensor_scalar(out=R_T, in0=R_VAR, scalar1=EPS,
                                              op0=ALU.add, scalar2=None))
                p.wait("vector", "sV", p.val("sV"))
                vop(lambda h: h.reciprocal(R_RC, R_T))
                p.wait("vector", "sV", p.val("sV"))
                vop(lambda h: h.tensor_tensor(out=R_VAR, in0=R_VAR, in1=R_RC,
                                              op=ALU.mult))
                p.wait("scalar", "sV", p.val("sV"))
                sop(lambda h: h.activation(out=R_T, in_=R_VAR, func=AF.Ln,
                                           bias=EPSC[0:1, 0:1]))
                p.wait("vector", "sS", p.val("sS"))
                vop(lambda h: h.tensor_tensor(out=R_RC, in0=R_L1, in1=R_T,
                                              op=ALU.add))
                p.wait("scalar", "sV", p.val("sV"))
                sop(lambda h, c=cs: h.activation(out=SROW[:, c], in_=R_RC,
                                                 func=AF.Exp, scale=-0.5))
                sop(lambda h, c=cs: h.activation(out=RHOROW[:, c], in_=R_T,
                                                 func=AF.Exp, scale=0.5))
            else:
                sop(lambda h, c=cs: h.activation(out=SROW[:, c], in_=R_L1,
                                                 func=AF.Exp, scale=-0.5))
            p.wait("vector", "sS", p.val("sS"))
            vop(lambda h, c=cs: h.tensor_tensor(out=BROW[:, c], in0=R_M,
                                                in1=SROW[:, c], op=ALU.mult))
            p.wait("vector", "sV", p.val("sV"))
            vop(lambda h, c=cs: h.tensor_scalar(out=BROW[:, c], in0=BROW[:, c],
                                                scalar1=-1.0, op0=ALU.mult,
                                                scalar2=None))
            ba, bb = (2, 3) if tc % 2 == 0 else (4, 5)
            p.wait("tensor", "sV", p.val("sV"))
            mm(BK[ba][:], ONESR[:], SROW[0:1, cs], True, True, [ba])
            mm(BK[bb][:], ONESR[:], BROW[0:1, cs], True, True, [bb])
            for k in range(KT):
                vop(lambda h, k=k, c=cs, b=ba: h.tensor_tensor(
                    out=TMPB[k % 2][:], in0=Xt(k)[:, c], in1=BK[b][:], op=ALU.mult),
                    read_banks=[ba])
                p.wait("vector", "sV", p.val("sV"))
                vop(lambda h, k=k, c=cs, b=bb: h.tensor_tensor(
                    out=XNt(k)[:, c], in0=TMPB[k % 2][:], in1=BK[b][:], op=ALU.add),
                    read_banks=[bb])

    ln_phase("ln1")
    tap("xn1", XN_ALL[:, 0:T], [128, T], F32)
    if not stage_on("inproj"):
        return finish()

    def load_wslice(src2d, m, slot, sem):
        ap = src2d[:, 128 * m:128 * (m + 1)].rearrange("(kt pk) m -> pk kt m",
                                                       pk=128)
        dst = WSL[slot][:].rearrange("pk (kt m) -> pk kt m", m=128)
        dma("gpsimd", dst, ap, sem)
        return p.val(sem)

    slot_pe = [0, 0]
    LN1_V = p.val("sV")
    p.wait("tensor", "sV", LN1_V)

    def inproj_like(src2d, dst_t, bias_mode, mrange):
        for m in mrange:
            slot = m % 2
            ssem = "sSl0" if slot == 0 else "sSl1"
            p.wait("gpsimd", "sPE", slot_pe[slot])
            dval = load_wslice(src2d, m, slot, ssem)
            if bias_mode == "bias2silu":
                bb = 4 + (m % 2)
                p.wait("tensor", ssem, dval)
                if m == 0:
                    p.wait("tensor", "sParG", PARG_TOT)
                for k in range(KT):
                    mm(BK[bb][:, m:m + 1], WSL[slot][:, 128 * k:128 * (k + 1)],
                       LNBBF[:, k:k + 1], k == 0, k == KT - 1, [bb])
                sop(lambda h, m=m, b=bb: h.activation(
                    out=BIAS2[:, m:m + 1], in_=BK[b][:, m:m + 1], func=AF.Identity,
                    bias=INB[:, m:m + 1], scale=1.0), read_banks=[bb])
                p.wait("gpsimd", ssem, dval)
                p.wait("gpsimd", "sPE", p.val("sPE"))
                for k in range(KT):
                    gop(lambda h, s=slot, k=k: h.tensor_scalar(
                        out=WSL[s][:, 128 * k:128 * (k + 1)],
                        in0=WSL[s][:, 128 * k:128 * (k + 1)],
                        scalar1=SM["LNG"][:, k:k + 1], op0=ALU.mult, scalar2=None))
                p.wait("tensor", "sG", p.val("sG"))
            else:
                p.wait("tensor", ssem, dval)
            for tc in range(4):
                bk = 6 if (m * 4 + tc) % 2 == 0 else 7
                cs = slice(512 * tc, 512 * (tc + 1))
                for k in range(KT):
                    mm(BK[bk][:], WSL[slot][:, 128 * k:128 * (k + 1)],
                       XNt(k)[:, cs], k == 0, k == KT - 1, [bk])
                if bias_mode == "bias2silu":
                    fn = AF.Identity if m < 8 else AF.Silu
                    sop(lambda h, m=m, c=cs, b=bk, f=fn: h.activation(
                        out=dst_t(m)[:, c], in_=BK[b][:], func=f,
                        bias=BIAS2[:, m:m + 1], scale=1.0), read_banks=[bk])
                else:
                    sop(lambda h, m=m, c=cs, b=bk: h.activation(
                        out=dst_t(m)[:, c], in_=BK[b][:], func=AF.Identity,
                        bias=SM["BBDT"][:, m:m + 1], scale=DTCOL[:, 0:1]),
                        read_banks=[bk])
            slot_pe[slot] = p.val("sPE")

    inproj_like(in_wT, XZt, "bias2silu", range(16))
    XP_DONE_S = p.val("sS")
    tap("xp0", XZ_ALL[:, 0:T], [128, T], F32)
    tap("z0", XZ_ALL[:, 8 * T:9 * T], [128, T], F32)
    if not stage_on("scan1"):
        return finish()

    def expand_dma(t, hb):
        mt, s = t // 16, t % 16
        src = XZt(mt)[8 * s:8 * s + 8, :].unsqueeze(1).broadcast_to([8, 16, T])
        sem = "sHin0" if hb == 0 else "sHin1"
        dma("sync", HBt(hb), src, sem)
        return sem, p.val(sem)

    hb_scan_done = [0, 0]
    ho_cons = [0, 0]
    p.wait("sync", "sS", XP_DONE_S)
    for t in range(128):
        hb = t % 2
        p.wait("sync", "sV", hb_scan_done[hb])
        sem, dv = expand_dma(t, hb)
        p.wait("vector", sem, dv)
        p.wait("vector", "sV", ho_cons[hb])
        vop(lambda h, t=t, s=hb: h.tensor_tensor_scan(
            out=HBt(2 + s), data0=PAR["ABAR"][:, t:t + 1].broadcast_to([128, T]),
            data1=HBt(s), initial=0.0, op0=ALU.mult, op1=ALU.add))
        hb_scan_done[hb] = p.val("sV")
        p.wait("vector", "sV", hb_scan_done[hb])
        vop(lambda h, t=t, s=hb: h.tensor_copy(PAR["SSELF"][:, t:t + 1],
                                               HBt(2 + s)[:, T - 1:T]))
        ho_cons[hb] = p.val("sV")
        if t == 5:
            tap("hexp5", HBt(hb), [128, T], F32)
            tap("hscan5", HBt(2 + hb), [128, T], F32)
    tap("sself", PAR["SSELF"][:], [128, 128], F32)
    if not stage_on("carry"):
        return finish()

    def allgather_carry(ssrc, ag_in, ag_out, width, logab_ap, hdst, sjbufs):
        p.wait("sync", "sV", p.val("sV"))
        dma("sync", ag_in.ap(), ssrc, "sAGi")
        p.wait("gpsimd", "sAGi", p.val("sAGi"))
        p.op("gpsimd", lambda h, ai=ag_in, ao=ag_out: h.collective_compute(
            "AllGather", ALU.bypass, replica_groups=[list(range(NC))],
            ins=[ai.ap().opt()], outs=[ao.ap().opt()]), inc="sAG")
        agv = p.val("sAG")
        p.wait("sync", "sAG", agv)
        sj_cons = [0, 0]
        for j in range(7):
            buf = sjbufs[j % 2]
            p.wait("sync", "sV", sj_cons[j % 2])
            dma("sync", buf, ag_out.ap()[128 * j:128 * (j + 1), :], "sAGo")
            do = p.val("sAGo")
            scr = PAR["SCR0"][:, 0:width]
            scr1 = PAR["SCR1"][:, 0:width]
            vop(lambda h, la=logab_ap, s=scr, j=j: h.tensor_scalar(
                out=s, in0=la, scalar1=SM["EV"][:, j:j + 1], op0=ALU.mult,
                scalar2=None))
            p.wait("scalar", "sV", p.val("sV"))
            sop(lambda h, s=scr: h.activation(out=s, in_=s, func=AF.Exp, scale=2048.0))
            p.wait("vector", "sS", p.val("sS"))
            p.wait("vector", "sAGo", do)
            vop(lambda h, s=scr, s1=scr1, b=buf: h.tensor_tensor(
                out=s1, in0=s, in1=b, op=ALU.mult))
            sj_cons[j % 2] = p.val("sV")
            p.wait("vector", "sV", p.val("sV"))
            vop(lambda h, s1=scr1, j=j, hd=hdst: h.scalar_tensor_tensor(
                out=hd, in0=s1, scalar=SM["MV"][:, j:j + 1], in1=hd,
                op0=ALU.mult, op1=ALU.add))

    allgather_carry(PAR["SSELF"][:], ag_a_in, ag_a_out, 128, PAR["LOGAB"][:],
                    PAR["HA"][:], [PAR["SJ0"][:], PAR["SJ1"][:]])
    tap("ha", PAR["HA"][:], [128, 128], F32)
    if not stage_on("scan2"):
        return finish()

    hb_scan_done = [0, 0]
    fold_done = [0, 0]
    mb_used = [0, 0]
    for P_ in range(8):
        ybanks = [0, 1, 2, 3] if P_ % 2 == 0 else [4, 5, 6, 7]
        for g in range(16):
            t = 16 * P_ + g
            hb = t % 2
            p.wait("sync", "sV", hb_scan_done[hb])
            sem, dv = expand_dma(t, hb)
            mbuf = MB[t % 2]
            p.wait("gpsimd", "sPE", mb_used[t % 2])
            if t >= 2:
                gprev = (t - 2) % 16
                gop(lambda h, mb=mbuf, gp=gprev: h.memset(
                    mb[:, 8 * gp:8 * gp + 8], 0.0))
            gop(lambda h, mb=mbuf, g=g, t=t: h.tensor_scalar(
                out=mb[:, 8 * g:8 * g + 8], in0=BEYE[:],
                scalar1=PAR["CBDT"][:, t:t + 1], op0=ALU.mult, scalar2=None))
            gv = p.val("sG")
            p.wait("vector", sem, dv)
            p.wait("vector", "sPE", fold_done[hb])
            vop(lambda h, t=t, s=hb: h.tensor_tensor_scan(
                out=HBt(2 + s), data0=PAR["ABAR"][:, t:t + 1].broadcast_to([128, T]),
                data1=HBt(s), initial=PAR["HA"][:, t:t + 1],
                op0=ALU.mult, op1=ALU.add))
            hb_scan_done[hb] = p.val("sV")
            scan_v = hb_scan_done[hb]
            p.wait("tensor", "sG", gv)
            p.wait("tensor", "sV", scan_v)
            for q in range(4):
                mm(BK[ybanks[q]][:], mbuf[:], HBt(2 + hb)[:, 512 * q:512 * (q + 1)],
                   g == 0, g == 15, [ybanks[q]], inc="always")
            fold_done[hb] = p.val("sPE")
            mb_used[t % 2] = p.val("sPE")
        for q in range(4):
            vop(lambda h, P_=P_, q=q, b=ybanks[q]: h.tensor_copy(
                XZt(P_)[:, 512 * q:512 * (q + 1)], BK[b][:]), read_banks=[ybanks[q]])
    tap("y0", XZ_ALL[:, 0:T], [128, T], F32)
    if not stage_on("attn_a"):
        return finish()

    p.wait("vector", "sS", XP_DONE_S)
    for k in range(KT):
        vop(lambda h, k=k: h.tensor_tensor(out=XZt(k), in0=XZt(k), in1=XZt(8 + k),
                                           op=ALU.mult))
    PROD_V = p.val("sV")
    p.wait("tensor", "sV", PROD_V)
    for tc in range(4):
        mm(BK[tc][:], ONESR[:], RHOROW[0:1, 512 * tc:512 * (tc + 1)], True, True,
           [tc])
    for m in range(KT):
        slot = m % 2
        ssem = "sSl0" if slot == 0 else "sSl1"
        p.wait("gpsimd", "sPE", slot_pe[slot])
        dval = load_wslice(out_wT, m, slot, ssem)
        p.wait("tensor", ssem, dval)
        for tc in range(4):
            bk = 5 if (m * 4 + tc) % 2 == 0 else 7
            cs = slice(512 * tc, 512 * (tc + 1))
            for k in range(KT):
                mm(BK[bk][:], WSL[slot][:, 128 * k:128 * (k + 1)],
                   XZt(k)[:, cs], k == 0, k == KT - 1, [bk])
            vop(lambda h, m=m, c=cs, tc=tc: h.tensor_tensor(
                out=TMPB[0][:], in0=XNt(m)[:, c], in1=BK[tc][:], op=ALU.mult),
                read_banks=[tc])
            p.wait("vector", "sV", p.val("sV"))
            vop(lambda h, m=m, b=bk: h.scalar_tensor_tensor(
                out=TMPB[1][:], in0=BK[b][:], scalar=SM["OUTB"][:, m:m + 1],
                in1=TMPB[0][:], op0=ALU.add, op1=ALU.add), read_banks=[bk])
            p.wait("vector", "sV", p.val("sV"))
            vop(lambda h: h.tensor_scalar(out=TMPB[1][:], in0=TMPB[1][:],
                                          scalar1=10.0, scalar2=-10.0,
                                          op0=ALU.min, op1=ALU.max))
            p.wait("vector", "sV", p.val("sV"))
            vop(lambda h, m=m, c=cs: h.tensor_tensor(
                out=Xt(m)[:, c], in0=Xt(m)[:, c], in1=TMPB[1][:], op=ALU.add))
        slot_pe[slot] = p.val("sPE")
    tap("x_attn_a", X_ALL[:, 0:T], [128, T], F32)
    if not stage_on("ln2"):
        return finish()

    def vopd(makefn, read_banks=()):
        p.wait("vector", "sV", p.val("sV"))
        return vop(makefn, read_banks)

    def moe_block(callidx):
        MOE_ENTRY_PE = p.val("sPE")
        MOE_ENTRY_V = p.val("sV")
        p.wait("tensor", "sV", p.val("sV"))
        for tc4 in range(4):
            cs = slice(512 * tc4, 512 * (tc4 + 1))
            bk = 0 if tc4 % 2 == 0 else 1
            if callidx == 0 and tc4 == 0:
                p.wait("tensor", "sParG", PARG_TOT)
            for k in range(KT):
                mm(BK[bk][0:32, :], GW[:, 32 * k:32 * (k + 1)], XNt(k)[:, cs],
                   k == 0, k == KT - 1, [bk])
            vopd(lambda h, c=cs, b=bk: h.tensor_copy(LOGFM[:, c], BK[b][0:32, :]),
                read_banks=[bk])
        p.wait("tensor", "sV", p.val("sV"))
        for j in range(16):
            bk = 2 if j % 2 == 0 else 3
            transpose(BK[bk][0:128, 0:32], LOGFM[:, 128 * j:128 * (j + 1)], [bk])
            sop(lambda h, j=j, b=bk: h.activation(
                out=SCORES[:, 32 * j:32 * (j + 1)], in_=BK[b][0:128, 0:32],
                func=AF.Sigmoid), read_banks=[bk])
        p.wait("vector", "sS", p.val("sS"))
        vopd(lambda h: h.tensor_tensor(
            out=SCALL.rearrange("p (a b) -> p a b", b=32),
            in0=SCORES.rearrange("p (a b) -> p a b", b=32),
            in1=GBF[:].unsqueeze(1).broadcast_to([128, 16, 32]), op=ALU.add))
        s4 = SCALL.rearrange("p (a g r) -> p a g r", g=8, r=4)
        pairs = [(0, 1), (0, 2), (0, 3), (1, 2), (1, 3), (2, 3)]
        for idx, (a, b) in enumerate(pairs):
            if idx == 0:
                vopd(lambda h, a=a, b=b: h.tensor_tensor(
                    out=GSALL.rearrange("p (x g) -> p x g", g=8),
                    in0=s4[:, :, :, a], in1=s4[:, :, :, b], op=ALU.add))
            else:
                vopd(lambda h, a=a, b=b: h.tensor_tensor(
                    out=IND.rearrange("p (x g) -> p x g", g=8),
                    in0=s4[:, :, :, a], in1=s4[:, :, :, b], op=ALU.add))
                p.wait("vector", "sV", p.val("sV"))
                vopd(lambda h: h.tensor_tensor(out=GSALL[:], in0=GSALL[:],
                                              in1=IND[:], op=ALU.max))
        for it in range(4):
            vopd(lambda h: h.tensor_reduce(
                M16[:], GSALL.rearrange("p (x g) -> p x g", g=8), axis=AX.X,
                op=ALU.max))
            p.wait("vector", "sV", p.val("sV"))
            vopd(lambda h: h.tensor_tensor(
                out=IND.rearrange("p (x g) -> p x g", g=8),
                in0=GSALL.rearrange("p (x g) -> p x g", g=8),
                in1=M16.unsqueeze(2).broadcast_to([128, 16, 8]), op=ALU.is_ge))
            p.wait("vector", "sV", p.val("sV"))
            if it == 0:
                vopd(lambda h: h.tensor_copy(GMASK[:], IND[:]))
            else:
                vopd(lambda h: h.tensor_tensor(out=GMASK[:], in0=GMASK[:],
                                              in1=IND[:], op=ALU.add))
            vopd(lambda h: h.scalar_tensor_tensor(
                out=GSALL[:], in0=IND[:], scalar=-1e30, in1=GSALL[:],
                op0=ALU.mult, op1=ALU.add))
        vopd(lambda h: h.tensor_tensor(
            out=MASKED.rearrange("p (x g r) -> p x g r", g=8, r=4),
            in0=s4,
            in1=GMASK.rearrange("p (x g) -> p x g", g=8).unsqueeze(3)
                .broadcast_to([128, 16, 8, 4]), op=ALU.mult))
        for it in range(8):
            vopd(lambda h: h.tensor_reduce(
                M16[:], MASKED.rearrange("p (x b) -> p x b", b=32), axis=AX.X,
                op=ALU.max))
            p.wait("vector", "sV", p.val("sV"))
            tgx = SEL if it == 0 else CWT
            vopd(lambda h, tg=tgx: h.tensor_tensor(
                out=tg.rearrange("p (x b) -> p x b", b=32),
                in0=MASKED.rearrange("p (x b) -> p x b", b=32),
                in1=M16.unsqueeze(2).broadcast_to([128, 16, 32]), op=ALU.is_ge))
            p.wait("vector", "sV", p.val("sV"))
            if it > 0:
                vopd(lambda h: h.tensor_tensor(out=SEL[:], in0=SEL[:], in1=CWT[:],
                                              op=ALU.add))
            vopd(lambda h, tg=tgx: h.scalar_tensor_tensor(
                out=MASKED[:], in0=tg[:], scalar=-1e30, in1=MASKED[:],
                op0=ALU.mult, op1=ALU.add))
        vopd(lambda h: h.tensor_tensor(out=CWT[:], in0=SCORES[:], in1=SEL[:],
                                      op=ALU.mult))
        vopd(lambda h: h.tensor_reduce(
            W16[:], CWT.rearrange("p (x b) -> p x b", b=32), axis=AX.X, op=ALU.add))
        vopd(lambda h: h.tensor_scalar(out=W16[:], in0=W16[:], scalar1=1e-20,
                                      op0=ALU.add, scalar2=None))
        p.wait("vector", "sV", p.val("sV"))
        vopd(lambda h: h.reciprocal(RC16[:], W16[:]))
        vopd(lambda h: h.tensor_scalar(out=RC16[:], in0=RC16[:], scalar1=2.5,
                                      op0=ALU.mult, scalar2=None))
        p.wait("vector", "sV", p.val("sV"))
        vopd(lambda h: h.tensor_tensor(
            out=CWT.rearrange("p (x b) -> p x b", b=32),
            in0=CWT.rearrange("p (x b) -> p x b", b=32),
            in1=RC16.unsqueeze(2).broadcast_to([128, 16, 32]), op=ALU.mult))
        p.wait("tensor", "sV", p.val("sV"))
        for j in range(16):
            bk = 2 if j % 2 == 0 else 3
            transpose(BK[bk][0:32, 0:128], CWT[:, 32 * j:32 * (j + 1)], [bk])
            vopd(lambda h, j=j, b=bk: h.tensor_copy(
                CFM[:, 128 * j:128 * (j + 1)], BK[b][0:32, 0:128]),
                read_banks=[bk])
        ROUTE_V = p.val("sV")
        tap(f"combine{callidx}", CFM, [32, T], F32)
        tap(f"scores{callidx}", SCORES, [128, 512], F32)
        tap(f"gmask{callidx}", GMASK, [128, 128], F32)
        tap(f"sel{callidx}", SEL, [128, 512], F32)
        tap(f"cwt{callidx}", CWT, [128, 512], F32)
        if callidx == 0 and not stage_on("ffn1"):
            return False
        p.wait("tensor", "sV", ROUTE_V)
        bcp_used = [ROUTE_V, ROUTE_V]
        bc_dv = [0, 0]
        gsb_used = [0, 0]
        prod_v = [0] * 4
        slot_free = [0] * 6
        wbase = {i: p.val(f"sW{i}") for i in range(6)}
        wring_pe = [max(MOE_ENTRY_PE, 1)] * 6
        wring_v = [MOE_ENTRY_V] * 6
        ldq = []
        for pr in range(16):
            ldq += [("g", 2 * pr), ("g", 2 * pr + 1), ("u", 2 * pr),
                    ("u", 2 * pr + 1), ("d", 2 * pr), ("d", 2 * pr + 1)]
        ldq += [("sg", 0), ("sg", 1), ("su", 0), ("su", 1), ("sd", 0), ("sd", 1)]
        nld = [0]

        def issue_load():
            if nld[0] >= len(ldq):
                return
            kind, idx = ldq[nld[0]]
            slotidx = nld[0] % 6
            sem = f"sW{slotidx}"
            p.wait("gpsimd", "sPE", wring_pe[slotidx])
            p.wait("gpsimd", "sV", wring_v[slotidx])
            if kind == "g":
                ap = wgT[idx].rearrange("(kt pk) i -> pk kt i", pk=128)
            elif kind == "u":
                ap = wuT[idx].rearrange("(kt pk) i -> pk kt i", pk=128)
            elif kind == "d":
                ap = wdnT[idx].rearrange("(it pk) m -> pk it m", pk=128)
            elif kind == "sg":
                ap = shgT[:, 512 * idx:512 * (idx + 1)].rearrange(
                    "(kt pk) i -> pk kt i", pk=128)
            elif kind == "su":
                ap = shuT[:, 512 * idx:512 * (idx + 1)].rearrange(
                    "(kt pk) i -> pk kt i", pk=128)
            else:
                ap = shdT[512 * idx:512 * (idx + 1), :].rearrange(
                    "(it pk) m -> pk it m", pk=128)
            nfree = ap.shape[1] * ap.shape[2]
            dst = WRslot(slotidx).rearrange("pk (a b) -> pk a b", a=ap.shape[1])
            dma("gpsimd", dst, ap, sem)
            nld[0] += 1

        for _ in range(6):
            issue_load()

        for pr in range(17):
            shared = (pr == 16)
            wv = {f"sW{i}": wbase[i] + 16 * (pr + 1) for i in range(6)}
            sl_g0, sl_g1, sl_u0, sl_u1, sl_d0, sl_d1 = range(6)
            def issue_bc(tc):
                cc = slice(256 * tc, 256 * (tc + 1))
                bsem = "sHin0" if tc % 2 == 0 else "sHin1"
                p.wait("sync", "sV", bcp_used[tc % 2])
                dma("sync", BCP[tc % 2][:, 0:256],
                    CFM[2 * pr:2 * pr + 1, cc].unsqueeze(1)
                    .broadcast_to([1, 128, 256]), bsem)
                dma("sync", BCP[tc % 2][:, 256:512],
                    CFM[2 * pr + 1:2 * pr + 2, cc].unsqueeze(1)
                    .broadcast_to([1, 128, 256]), bsem)
                bc_dv[tc % 2] = p.val(bsem)

            for tc in range(8):
                c2 = slice(256 * tc, 256 * (tc + 1))
                bsem = "sHin0" if tc % 2 == 0 else "sHin1"
                if not shared:
                    if tc == 0:
                        issue_bc(0)
                    if tc < 7:
                        issue_bc(tc + 1)
                bcdv = bc_dv[tc % 2]
                for it in range(4):
                    bG = 0 if it % 2 == 0 else 2
                    bU = 1 if it % 2 == 0 else 3
                    p.wait("tensor", "sW0", wv["sW0"])
                    p.wait("tensor", "sW1", wv["sW1"])
                    p.bank_write([bG])
                    for k in range(KT):
                        mm(BK[bG][:, 0:256],
                           WRslot(sl_g0)[:, 512 * k + 128 * it:512 * k + 128 * (it + 1)],
                           XNt(k)[:, c2], k == 0, False, [bG], inc=None)
                    for k in range(KT):
                        vv = mm(BK[bG][:, 256:512],
                                WRslot(sl_g1)[:, 512 * k + 128 * it:512 * k + 128 * (it + 1)],
                                XNt(k)[:, c2], k == 0, k == KT - 1, [bG],
                                inc=("always" if k == KT - 1 else None))
                    p.bank_produced([bG], vv)
                    if tc == 7 and it == 3:
                        slot_free[sl_g0] = slot_free[sl_g1] = vv
                    p.wait("scalar", "sV", gsb_used[it % 2])
                    sop(lambda h, b=bG, pp=it % 2: h.activation(
                        out=GSB[pp][:], in_=BK[b][:], func=AF.Silu), read_banks=[bG])
                    gsv = p.val("sS")
                    p.wait("tensor", "sW2", wv["sW2"])
                    p.wait("tensor", "sW3", wv["sW3"])
                    p.bank_write([bU])
                    for k in range(KT):
                        mm(BK[bU][:, 0:256],
                           WRslot(sl_u0)[:, 512 * k + 128 * it:512 * k + 128 * (it + 1)],
                           XNt(k)[:, c2], k == 0, False, [bU], inc=None)
                    for k in range(KT):
                        vv = mm(BK[bU][:, 256:512],
                                WRslot(sl_u1)[:, 512 * k + 128 * it:512 * k + 128 * (it + 1)],
                                XNt(k)[:, c2], k == 0, k == KT - 1, [bU],
                                inc=("always" if k == KT - 1 else None))
                    p.bank_produced([bU], vv)
                    p.wait("vector", "sS", gsv)
                    vop(lambda h, b=bU, pp=it % 2: h.tensor_tensor(
                        out=TBF[pp][:], in0=GSB[pp][:], in1=BK[b][:], op=ALU.mult),
                        read_banks=[bU])
                    gsb_used[it % 2] = p.val("sV")
                    p.wait("vector", "sV", p.val("sV"))
                    if shared:
                        vop(lambda h, it=it, pp=it % 2: h.tensor_copy(
                            PRB[it][:], TBF[pp][:]))
                        prod_v[it] = p.val("sV")
                    else:
                        p.wait("vector", bsem, bcdv)
                        vop(lambda h, it=it, pp=it % 2, bp=tc % 2: h.tensor_tensor(
                            out=PRB[it][:], in0=TBF[pp][:], in1=BCP[bp][:],
                            op=ALU.mult))
                        bcp_used[tc % 2] = p.val("sV")
                    prod_v[it] = p.val("sV")
                PR_V = p.val("sV")
                p.wait("tensor", "sW4", wv["sW4"])
                p.wait("tensor", "sW5", wv["sW5"])
                p.bank_write([4, 5, 6, 7])
                last_v = 0
                for it in range(4):
                    p.wait("tensor", "sV", prod_v[it])
                    for half, sld in ((0, sl_d0), (1, sl_d1)):
                        for m in range(KT):
                            bD = 4 + m // 2
                            dsl = BK[bD][:, 256 * (m % 2):256 * (m % 2) + 256]
                            first = (it == 0 and half == 0 and m % 2 == 0)
                            last = (it == 3 and half == 1)
                            last_v = mm(dsl,
                                        WRslot(sld)[:, 1024 * it + 128 * m:
                                                    1024 * it + 128 * (m + 1)],
                                        PRB[it][:, 256 * half:256 * (half + 1)],
                                        first, last, [bD],
                                        inc=("always" if last else None))
                p.bank_produced([4, 5, 6, 7], last_v)
                if tc == 7:
                    slot_free[sl_d0] = slot_free[sl_d1] = last_v
                for bb_ in range(4):
                    m0 = 2 * bb_
                    xap = X_ALL[:, T * m0:T * (m0 + 2)].rearrange(
                        "p (m t) -> p m t", m=2)[:, :, 256 * tc:256 * (tc + 1)]
                    vop(lambda h, b=4 + bb_, xa=xap: h.tensor_tensor(
                        out=xa, in0=xa,
                        in1=BK[b][:].rearrange("p (m t) -> p m t", m=2),
                        op=ALU.add), read_banks=[4 + bb_])
            done_v = p.val("sV")
            for i in range(6):
                wring_pe[i] = slot_free[i]
                wring_v[i] = done_v
            for _ in range(6):
                issue_load()
        return True

    ln_phase("plain")
    tap("hn1", XN_ALL[:, 0:T], [128, T], F32)
    if not stage_on("routing1"):
        return finish()
    cont = moe_block(0)
    tap("x_ffn1", X_ALL[:, 0:T], [128, T], F32)
    if not cont or not stage_on("attn_b"):
        return finish()

    ln_phase("plain")
    inproj_like(wbT, XZt, "bu", range(8))
    BU_S = p.val("sS")
    p.wait("vector", "sS", BU_S)
    for k in range(KT):
        vop(lambda h, k=k: h.tensor_tensor_scan(
            out=HBt(2), data0=SM["ABARB"][:, k:k + 1].broadcast_to([128, T]),
            data1=XZt(k), initial=0.0, op0=ALU.mult, op1=ALU.add))
        p.wait("vector", "sV", p.val("sV"))
        vop(lambda h, k=k: h.tensor_copy(SM["SBB"][:, k:k + 1], HBt(2)[:, T - 1:T]))
    allgather_carry(SM["SBB"][:], ag_b_in, ag_b_out, 8, SM["LOGABB"][:],
                    SM["HBB"][:], [SM["SJB"][:], SM["SJB2"][:]])
    for k in range(KT):
        vop(lambda h, k=k: h.tensor_tensor_scan(
            out=XZt(8 + k), data0=SM["ABARB"][:, k:k + 1].broadcast_to([128, T]),
            data1=XZt(k), initial=SM["HBB"][:, k:k + 1], op0=ALU.mult, op1=ALU.add))
    XS_V = p.val("sV")
    tap("xs0", XZ_ALL[:, 8 * T:9 * T], [128, T], F32)
    p.wait("tensor", "sV", XS_V)
    for m in range(KT):
        p.wait("gpsimd", "sPE", slot_pe[0])
        dc = load_wslice(wcT, m, 0, "sSl0")
        p.wait("gpsimd", "sPE", slot_pe[1])
        dd = load_wslice(wdT, m, 1, "sSl1")
        p.wait("tensor", "sSl0", dc)
        p.wait("tensor", "sSl1", dd)
        for tc in range(4):
            bk = 6 if (m * 4 + tc) % 2 == 0 else 7
            cs = slice(512 * tc, 512 * (tc + 1))
            for k in range(KT):
                mm(BK[bk][:], WSL[0][:, 128 * k:128 * (k + 1)], XZt(8 + k)[:, cs],
                   k == 0, False, [bk], inc=None)
            for k in range(KT):
                vv = mm(BK[bk][:], WSL[1][:, 128 * k:128 * (k + 1)], XNt(k)[:, cs],
                        False, k == KT - 1, [bk],
                        inc=("always" if k == KT - 1 else None))
            p.bank_produced([bk], vv)
            vop(lambda h, m=m, c=cs, b=bk: h.scalar_tensor_tensor(
                out=Xt(m)[:, c], in0=BK[b][:], scalar=SM["BCD"][:, m:m + 1],
                in1=Xt(m)[:, c], op0=ALU.add, op1=ALU.add), read_banks=[bk])
        slot_pe[0] = p.val("sPE")
        slot_pe[1] = p.val("sPE")
    tap("x_attn_b", X_ALL[:, 0:T], [128, T], F32)
    if not stage_on("ffn2"):
        return finish()

    ln_phase("plain")
    moe_block(1)
    tap("x_ffn2", X_ALL[:, 0:T], [128, T], F32)
    return finish()


def _prep_inputs(inputs):
    f32 = np.float32

    def c(a):
        return np.ascontiguousarray(a, dtype=f32)

    x = np.asarray(inputs["L0_attn_a_x"], dtype=f32)[0]
    com = {
        "in_wT": c(np.asarray(inputs["L0_attn_a_in_w"], dtype=f32).T),
        "out_wT": c(np.asarray(inputs["L0_attn_a_out_w"], dtype=f32).T),
        "wbT": c(np.asarray(inputs["L0_attn_b_W_B"], dtype=f32).T),
        "wcT": c(np.asarray(inputs["L0_attn_b_W_C"], dtype=f32).T),
        "wdT": c(np.asarray(inputs["L0_attn_b_W_D"], dtype=f32).T),
        "gate_wT": c(np.asarray(inputs["L0_ffn_gate_w"], dtype=f32).T),
        "wgT": c(np.asarray(inputs["L0_ffn_w_gate"], dtype=f32).transpose(0, 2, 1)),
        "wuT": c(np.asarray(inputs["L0_ffn_w_up"], dtype=f32).transpose(0, 2, 1)),
        "wdnT": c(np.asarray(inputs["L0_ffn_w_down"], dtype=f32).transpose(0, 2, 1)),
        "shgT": c(np.asarray(inputs["L0_ffn_sh_gate"], dtype=f32).T),
        "shuT": c(np.asarray(inputs["L0_ffn_sh_up"], dtype=f32).T),
        "shdT": c(np.asarray(inputs["L0_ffn_sh_down"], dtype=f32).T),
        "A_all": c(np.asarray(inputs["L0_attn_a_A_log"], dtype=f32)
                   .reshape(128, 128).T),
        "B_all": c(np.asarray(inputs["L0_attn_a_B_param"], dtype=f32)
                   .reshape(128, 128).T),
        "C_all": c(np.asarray(inputs["L0_attn_a_C_param"], dtype=f32)
                   .reshape(128, 128).T),
        "ident_in": np.eye(128, dtype=f32),
        "beye_in": (np.arange(128)[:, None] // 16 ==
                    np.arange(8)[None, :]).astype(f32),
        "gate_b_r": c(np.asarray(inputs["L0_ffn_gate_bias"], dtype=f32)
                      .reshape(1, E)),
        "logdt_b": c(np.asarray(inputs["L0_attn_b_log_dt"], dtype=f32)
                     .reshape(1, 1)),
    }
    ld = np.asarray(inputs["L0_attn_a_log_dt"], dtype=f32)
    idx = (8 * np.arange(128))[None, :] + (np.arange(128) // 16)[:, None]
    com["LDT_all"] = c(ld[idx])

    def col128(v):
        return c(np.asarray(v, dtype=f32).reshape(-1, 128).T)

    com["in_b_c"] = col128(inputs["L0_attn_a_in_b"])
    com["out_b_c"] = col128(inputs["L0_attn_a_out_b"])
    com["ln_g_c"] = col128(inputs["L0_attn_a_ln_g"])
    com["ln_b_c"] = col128(inputs["L0_attn_a_ln_b"])
    com["logA_b_c"] = col128(inputs["L0_attn_b_log_A"])
    com["bB_c"] = col128(inputs["L0_attn_b_b_B"])
    com["bC_c"] = col128(inputs["L0_attn_b_b_C"])
    com["bD_c"] = col128(inputs["L0_attn_b_b_D"])

    in_maps = []
    for cix in range(NC):
        m = dict(com)
        m["x_in"] = c(x[T * cix:T * (cix + 1), :].T)
        ev = np.zeros((128, 8), dtype=f32)
        mv = np.zeros((128, 8), dtype=f32)
        for j in range(7):
            if j < cix:
                ev[:, j] = cix - 1 - j
                mv[:, j] = 1.0
        m["e_vec"] = ev
        m["m_vec"] = mv
        in_maps.append(m)
    return in_maps


def kernel(**inputs):
    in_maps = _prep_inputs(inputs)
    nc, _ = build("full")
    res = run_bass_kernel_spmd(nc, in_maps, core_ids=list(range(NC)))
    parts = [np.asarray(r["y_out"]).T for r in res.results]
    out = np.concatenate(parts, axis=0).reshape(1, NC * T, D)
    return np.ascontiguousarray(out, dtype=np.float32)
